# version 09
# baseline (speedup 1.0000x reference)
import contextlib
import numpy as np
import ml_dtypes
import concourse.bass as bass
import concourse.mybir as mybir
from concourse.bass_utils import run_bass_kernel_spmd

F32 = mybir.dt.float32
BF16 = mybir.dt.bfloat16
AF = mybir.ActivationFunctionType
ALU = mybir.AluOpType
AX = mybir.AxisListType
BFD = ml_dtypes.bfloat16

ENGINES = ("tensor", "vector", "scalar", "gpsimd", "sync")

D = 1024
LQ = 2048
NT = 16
NCD = 8
SEQ = 8192
NKT = 64
NPAIR = 8
DFF = 4096
EPS = 1e-5


class Sem:
    def __init__(self, handle):
        self.h = handle
        self.n = 0

    def inc(self, k=1):
        self.n += k
        return self.n


class Prog:
    def __init__(self):
        self.items = {e: [] for e in ENGINES}

    def on(self, eng, fn):
        self.items[eng].append(fn)

    def materialize(self, block):
        for e in ENGINES:
            fns = self.items[e]

            def run(engine, fns=fns):
                for fn in fns:
                    fn(engine)

            getattr(block, e)(run)


def build_program(dump=(), upto=8):
    nc = bass.Bass("TRN2", target_bir_lowering=False)
    p = Prog()

    def ein(name, shape, dtype=F32):
        return nc.dram_tensor(name, shape, dtype, kind="ExternalInput").ap()

    x_in = ein("x", [LQ, D])
    masks_in = ein("masks", [16, 128, 512])
    qpd_in = ein("qpd", [128, LQ])
    wqT0 = ein("wqT0", [D, D]); wkT0 = ein("wkT0", [D, D]); wvT0 = ein("wvT0", [D, D])
    woT0 = ein("woT0", [D, D])
    bq0 = ein("bq0", [128, NCD]); bk0 = ein("bk0", [128, NCD]); bv0 = ein("bv0", [128, NCD])
    bvr0 = ein("bvr0", [1, D])
    bo0 = ein("bo0", [1, D])
    gamma_in = ein("gamma", [1, 16])
    w1b = ein("w1b", [8, D, 512]); w2b = ein("w2b", [8, D, 512])
    h1b = ein("h1b", [8, DFF, 512]); h2b = ein("h2b", [8, DFF, 512])
    w3T_in = ein("w3T", [DFF, D])
    s1_in = ein("s1", [128, 32]); s2_in = ein("s2", [128, 32])
    s3_in = ein("s3", [128, 32]); s4_in = ein("s4", [128, 32])
    s5_in = ein("s5", [1, D])
    wq1 = ein("wq1", [D, D]); wk1 = ein("wk1", [D, D]); wv1 = ein("wv1", [D, D])
    wo1 = ein("wo1", [D, D]); bo1 = ein("bo1", [1, D])
    wt1 = ein("wt1", [D, D]); uat = ein("uat", [50, D]); lup = ein("lup", [50, 4])
    ldn = ein("ldn", [4, 100]); dax = ein("dax", [100, D]); b1_in = ein("b1", [1, D])

    out_d = nc.dram_tensor("out", [LQ, D], F32, kind="ExternalOutput").ap()

    def din(name, shape, dtype=BF16):
        return nc.dram_tensor(name, shape, dtype, kind="Internal").ap()

    _pad = nc.dram_tensor("dram_pad", [64 * 1024 * 1024], mybir.dt.uint8, kind="Internal")
    qTd = din("qTd", [D, LQ])
    kin = din("kin", [D, LQ]); vinT = din("vinT", [8, LQ, 128])
    kg8 = din("kg8", [8, 4 * 128, LQ]); vgT8 = din("vgT8", [8, 4, LQ, 128])
    aoTd = din("aoTd", [D, LQ])
    w1c = din("w1c", [8, D, 512]); w2c = din("w2c", [8, D, 512])
    h1c = din("h1c", [8, DFF, 512]); h2c = din("h2c", [8, DFF, 512])
    w3c = din("w3c", [DFF, D])

    dump_outs = {}
    for name in dump:
        shp, dt = {
            "lnT": ([128, NCD * LQ], F32), "qT": ([D, LQ], F32),
            "kg": ([4 * D, LQ], F32), "vg": ([4 * D, LQ], F32),
            "aoT": ([D, LQ], F32), "vin": ([8, LQ, 128], F32),
            "x1": ([LQ, D], F32), "x2": ([LQ, D], F32),
            "x3": ([LQ, D], F32),
        }[name]
        dump_outs[name] = nc.dram_tensor("dbg_" + name, shp, dt, kind="ExternalOutput").ap()

    ident_c = nc.inline_tensor(np.eye(128, dtype=BFD), "ident_c").ap()
    ones_row_c = nc.inline_tensor(np.ones((1, 128), np.float32), "ones_row_c").ap()
    sel_lo = np.zeros((1, 128), np.float32); sel_lo[0, :64] = 1.0
    sel_lo_c = nc.inline_tensor(sel_lo, "sel_lo_c").ap()
    sel_hi = np.zeros((1, 128), np.float32); sel_hi[0, 64:] = 1.0
    sel_hi_c = nc.inline_tensor(sel_hi, "sel_hi_c").ap()

    es = contextlib.ExitStack()
    with es:
        tms = Sem(es.enter_context(nc.semaphore("tms")))
        vms = Sem(es.enter_context(nc.semaphore("vms")))
        ams = Sem(es.enter_context(nc.semaphore("ams")))
        stx = Sem(es.enter_context(nc.semaphore("stx")))
        ccK = [[Sem(es.enter_context(nc.semaphore(f"ccK{l}_{c}"))) for c in range(8)]
               for l in range(2)]
        ccV = [[Sem(es.enter_context(nc.semaphore(f"ccV{l}_{c}"))) for c in range(8)]
               for l in range(2)]
        def mksem(name):
            return Sem(es.enter_context(nc.semaphore(name)))
        sX = mksem("sX"); sCst = mksem("sCst"); sMask = mksem("sMask")
        gMask = mksem("gMask")
        sCache = mksem("sCache"); sOut = mksem("sOut")
        gW0 = mksem("gW0"); gW1 = mksem("gW1"); gLW = mksem("gLW")
        aSmall = mksem("aSmall"); aBo0 = mksem("aBo0"); aBo1 = mksem("aBo1")
        aScl = mksem("aScl"); aB1 = mksem("aB1")
        projS = [mksem(f"projS{i}") for i in range(2)]
        kbS = [mksem(f"kbS{i}") for i in range(2)]
        vbS = [mksem(f"vbS{i}") for i in range(2)]
        qS = [mksem(f"qS{i}") for i in range(2)]
        stgS = [mksem(f"stgS{i}") for i in range(2)]
        aotS = mksem("aotS")
        wbsS = [mksem(f"wbsS{i}") for i in range(2)]
        w3S = [mksem(f"w3S{i}") for i in range(2)]
        vtS = [mksem(f"vtS{i}") for i in range(8)]

        xs = es.enter_context(nc.sbuf_tensor("xs", [128, NT * 1024], F32))
        cst = es.enter_context(nc.sbuf_tensor("cst", [128, 4], F32))
        ident = es.enter_context(nc.sbuf_tensor("ident", [128, 128], BF16))
        selr = es.enter_context(nc.sbuf_tensor("selr", [1, 384], F32))
        biasq = es.enter_context(nc.sbuf_tensor("biasq", [128, 3 * NCD], F32))
        bvrow_sb = es.enter_context(nc.sbuf_tensor("bvrow", [1, D], F32))
        bvbc = es.enter_context(nc.sbuf_tensor("bvbc_v", [128, D], F32))
        gam = es.enter_context(nc.sbuf_tensor("gam", [1, 32], F32))
        lgp = es.enter_context(nc.sbuf_tensor("lgp", [1, 16], F32))
        lngpair = es.enter_context(nc.sbuf_tensor("lngpair", [128, 8], F32))
        lnT = es.enter_context(nc.sbuf_tensor("lnT", [128, NCD * LQ], BF16))

        sb_x3 = xs[:].rearrange("p (t i) -> p t i", t=NT)
        ones_r = selr[0:1, 0:128]
        sel_l = selr[0:1, 128:256]
        sel_h = selr[0:1, 256:384]

        def _(sy):
            sy.dma_start(out=sb_x3, in_=x_in.rearrange("(t p) i -> p t i", p=128)).then_inc(sX.h, 16)
            sy.dma_start(out=ident[:], in_=ident_c).then_inc(sCst.h, 16)
            sy.dma_start(out=selr[:, 0:128], in_=ones_row_c).then_inc(sCst.h, 16)
            sy.dma_start(out=selr[:, 128:256], in_=sel_lo_c).then_inc(sCst.h, 16)
            sy.dma_start(out=selr[:, 256:384], in_=sel_hi_c).then_inc(sCst.h, 16)
        p.on("sync", _)
        s_x = sX.inc(16)
        s_const = sCst.inc(64)

        def _(a):
            for dst, src in [(biasq[:, 0:NCD], bq0), (biasq[:, NCD:2 * NCD], bk0),
                             (biasq[:, 2 * NCD:3 * NCD], bv0), (gam[:, 0:16], gamma_in)]:
                a.dma_start(out=dst, in_=src).then_inc(aSmall.h, 16)
            a.dma_start(out=bvrow_sb[:], in_=bvr0).then_inc(aSmall.h, 16)
        p.on("scalar", _)
        a_small = aSmall.inc(16 * 5)

        def _(v):
            v.memset(cst[:, 0:1], EPS)
            v.memset(cst[:, 1:2], 0.125).then_inc(vms.h, 1)
        p.on("vector", _)
        v_cst = vms.inc()

        def emit_ln(tag, esc):
            sums = esc.enter_context(nc.sbuf_tensor(f"lns_{tag}", [128, 4 * NT], F32))
            sqs = esc.enter_context(nc.sbuf_tensor(f"lnq_{tag}", [128, 1024], F32))
            sumsq = esc.enter_context(nc.sbuf_tensor(f"lnsq_{tag}", [128, NT], F32))
            lntm = esc.enter_context(nc.sbuf_tensor(f"lntm_{tag}", [128, 2 * 1024], BF16))
            tps = [esc.enter_context(nc.psum_tensor(f"lntp{i}_{tag}", [128, 512], BF16))
                   for i in range(2)]
            m = sums[:, NT:2 * NT]; var = sums[:, 2 * NT:3 * NT]; rstd = sums[:, 3 * NT:4 * NT]
            x_entry_v = vms.n
            is1 = tag == "ln1"

            def _(v, x_entry_v=x_entry_v):
                if is1:
                    v.wait_ge(sX.h, s_x)
                else:
                    v.wait_ge(vms.h, x_entry_v)
                v.tensor_reduce(out=sums[:, 0:NT], in_=sb_x3, axis=AX.X, op=ALU.add).then_inc(vms.h, 1)
            p.on("vector", _)
            v_sums = vms.inc()

            base_a = ams.n

            def _(a, x_entry_v=x_entry_v, base_a=base_a):
                if is1:
                    a.wait_ge(sX.h, s_x)
                else:
                    a.wait_ge(vms.h, x_entry_v)
                for t in range(NT):
                    if t:
                        a.wait_ge(ams.h, base_a + t)
                    a.activation(out=sqs[:], in_=xs[:, 1024 * t:1024 * (t + 1)],
                                 func=AF.Square,
                                 accum_out=sumsq[:, t:t + 1]).then_inc(ams.h, 1)
            p.on("scalar", _)
            ams.inc(NT - 1)
            a_sq = ams.inc()

            base_v = vms.n

            def _(v, a_sq=a_sq, base_v=base_v, v_sums=v_sums):
                v.wait_ge(ams.h, a_sq)
                v.wait_ge(vms.h, v_sums)
                v.tensor_scalar_mul(out=m, in0=sums[:, 0:NT], scalar1=1.0 / 1024.0).then_inc(vms.h, 1)
                v.tensor_scalar_mul(out=var, in0=sumsq[:], scalar1=1.0 / 1024.0).then_inc(vms.h, 1)
                v.wait_ge(vms.h, base_v + 2)
                v.tensor_mul(out=sumsq[:], in0=m, in1=m).then_inc(vms.h, 1)
                v.wait_ge(vms.h, base_v + 3)
                v.tensor_sub(out=var, in0=var, in1=sumsq[:]).then_inc(vms.h, 1)
            p.on("vector", _)
            vms.inc(3)
            v_var = vms.inc()

            def _(a, v_var=v_var):
                a.wait_ge(vms.h, max(v_var, v_cst))
                a.activation(out=rstd, in_=var, func=AF.Sqrt, bias=cst[:, 0:1]).then_inc(ams.h, 1)
            p.on("scalar", _)
            a_rt = ams.inc()

            base_v2 = vms.n

            def _(v, a_rt=a_rt, base_v2=base_v2):
                v.wait_ge(ams.h, a_rt)
                v.reciprocal(out=rstd, in_=rstd).then_inc(vms.h, 1)
                v.wait_ge(vms.h, base_v2 + 1)
                v.tensor_mul(out=m, in0=m, in1=rstd).then_inc(vms.h, 1)
                v.wait_ge(vms.h, base_v2 + 2)
                v.tensor_scalar_mul(out=m, in0=m, scalar1=-1.0).then_inc(vms.h, 1)
            p.on("vector", _)
            vms.inc(2)
            v_stat = vms.inc()

            v_aff = []
            tr_last = []
            evs = []
            for t in range(NT):
                warT = tr_last[t - 2] if t >= 2 else 0

                def _(v, t=t, warT=warT):
                    if t == 0:
                        v.wait_ge(vms.h, v_stat)
                    if warT:
                        v.wait_ge(tms.h, warT)
                    v.tensor_scalar(
                        out=lntm[:, 1024 * (t % 2):1024 * (t % 2 + 1)],
                        in0=xs[:, 1024 * t:1024 * (t + 1)],
                        scalar1=rstd[:, t:t + 1], scalar2=m[:, t:t + 1],
                        op0=ALU.mult, op1=ALU.add).then_inc(vms.h, 1)
                p.on("vector", _)
                v_aff.append(vms.inc())

                for g in range(2):
                    gi = t * 2 + g
                    bank = tps[gi % 2]
                    war = evs[gi - 2] if gi >= 2 else 0

                    def _(te, t=t, g=g, bank=bank, war=war):
                        te.wait_ge(vms.h, v_aff[t])
                        if t == 0 and g == 0:
                            te.wait_ge(sCst.h, s_const)
                        if war:
                            te.wait_ge(vms.h, war)
                        for ci in range(4):
                            c = g * 4 + ci
                            ins = te.transpose(
                                bank[:, 128 * ci:128 * (ci + 1)],
                                lntm[:, 1024 * (t % 2) + 128 * c:1024 * (t % 2) + 128 * (c + 1)],
                                ident[:])
                        ins.then_inc(tms.h, 1)
                    p.on("tensor", _)
                    t_tr = tms.inc()

                    def _(v, t=t, g=g, bank=bank, t_tr=t_tr):
                        v.wait_ge(tms.h, t_tr)
                        for ci in range(4):
                            c = g * 4 + ci
                            ins = v.tensor_copy(
                                out=lnT[:, LQ * c + 128 * t:LQ * c + 128 * (t + 1)],
                                in_=bank[:, 128 * ci:128 * (ci + 1)])
                        ins.then_inc(vms.h, 1)
                    p.on("vector", _)
                    evs.append(vms.inc())
                tr_last.append(t_tr)
            return vms.n

        proj_ev = []

        def emit_proj(wT_sb, out_dram, bias_ap, pss, v_lnT, w_sem, w_val, stage,
                      chunk_stores=None):
            if chunk_stores is None:
                chunk_stores = []
            ev_done = proj_ev
            first = [True]
            for c in range(NCD):
                slot = c % 2
                stage_war = projS[slot].n
                for s in range(4):
                    idx = len(ev_done)
                    bank = pss[(c * 4 + s) % 2]
                    warv = ev_done[idx - 2] if idx >= 2 else 0
                    fst = first[0]; first[0] = False

                    def _(te, c=c, s=s, bank=bank, warv=warv, fst=fst):
                        if fst:
                            te.wait_ge(vms.h, v_lnT)
                            te.wait_ge(w_sem.h, w_val)
                        if warv:
                            te.wait_ge(vms.h, warv)
                        for ic in range(NCD):
                            ins = te.matmul(
                                bank[:],
                                wT_sb[:, D * ic + 128 * c:D * ic + 128 * (c + 1)],
                                lnT[:, LQ * ic + 512 * s:LQ * ic + 512 * (s + 1)],
                                start=(ic == 0), stop=(ic == NCD - 1))
                        ins.then_inc(tms.h, 1)
                    p.on("tensor", _)
                    t_mm = tms.inc()

                    def _(v, c=c, s=s, bank=bank, t_mm=t_mm, stage_war=stage_war, slot=slot):
                        v.wait_ge(tms.h, t_mm)
                        if s == 0:
                            if stage_war:
                                v.wait_ge(projS[slot].h, stage_war)
                            if bias_ap is not None:
                                v.wait_ge(aSmall.h, a_small)
                        if bias_ap is not None:
                            v.tensor_scalar_add(
                                out=stage[:, LQ * slot + 512 * s:LQ * slot + 512 * (s + 1)],
                                in0=bank[:], scalar1=bias_ap[:, c:c + 1]).then_inc(vms.h, 1)
                        else:
                            v.tensor_copy(
                                out=stage[:, LQ * slot + 512 * s:LQ * slot + 512 * (s + 1)],
                                in_=bank[:]).then_inc(vms.h, 1)
                    p.on("vector", _)
                    ev_done.append(vms.inc())

                def _(sy, c=c, slot=slot, ev=ev_done[-1]):
                    sy.wait_ge(vms.h, ev)
                    sy.dma_start(out=out_dram[128 * c:128 * (c + 1), :],
                                 in_=stage[:, LQ * slot:LQ * (slot + 1)]).then_inc(projS[slot].h, 16)
                p.on("sync", _)
                chunk_stores.append((projS[slot], projS[slot].inc(16)))
            return (projS[0].n, projS[1].n)

        def emit_projT(wsb, bvbc, pss, v_lnT, w_sem, w_val, stage, chunk_stores):
            ev_done = proj_ev
            first = [True]
            for n in range(2):
                for tg in range(4):
                    slot = (n * 4 + tg) % 2
                    stage_war = projS[slot].n
                    for ti in range(4):
                        t = 4 * tg + ti
                        idx = len(ev_done)
                        bank = pss[idx % 2]
                        warv = ev_done[idx - 2] if idx >= 2 else 0
                        fst = first[0]; first[0] = False

                        def _(te, n=n, t=t, bank=bank, warv=warv, fst=fst):
                            if fst:
                                te.wait_ge(vms.h, v_lnT)
                                te.wait_ge(w_sem.h, w_val)
                            if warv:
                                te.wait_ge(vms.h, warv)
                            for ic in range(NCD):
                                ins = te.matmul(
                                    bank[:],
                                    lnT[:, LQ * ic + 128 * t:LQ * ic + 128 * (t + 1)],
                                    wsb[:, D * ic + 512 * n:D * ic + 512 * (n + 1)],
                                    start=(ic == 0), stop=(ic == NCD - 1))
                            ins.then_inc(tms.h, 1)
                        p.on("tensor", _)
                        t_mm = tms.inc()

                        def _(v, n=n, ti=ti, bank=bank, t_mm=t_mm,
                              stage_war=stage_war, slot=slot):
                            v.wait_ge(tms.h, t_mm)
                            if ti == 0 and stage_war:
                                v.wait_ge(projS[slot].h, stage_war)
                            if bvbc is not None:
                                v.tensor_tensor(
                                    out=stage[:, LQ * slot + 512 * ti:LQ * slot + 512 * (ti + 1)],
                                    in0=bank[:], in1=bvbc[:, 512 * n:512 * (n + 1)],
                                    op=ALU.add).then_inc(vms.h, 1)
                            else:
                                v.tensor_copy(
                                    out=stage[:, LQ * slot + 512 * ti:LQ * slot + 512 * (ti + 1)],
                                    in_=bank[:]).then_inc(vms.h, 1)
                        p.on("vector", _)
                        ev_done.append(vms.inc())

                    def _(sy, n=n, tg=tg, slot=slot, ev=ev_done[-1]):
                        sy.wait_ge(vms.h, ev)
                        sview = stage[:, LQ * slot:LQ * slot + 2048].rearrange(
                            "p (ti q e) -> p ti q e", ti=4, q=4)
                        for q in range(4):
                            sy.dma_start(
                                out=vinT[4 * n + q, 512 * tg:512 * (tg + 1), :].rearrange(
                                    "(ti pp) e -> pp ti e", pp=128),
                                in_=sview[:, :, q, :]).then_inc(projS[slot].h, 16)
                    p.on("sync", _)
                    projS[slot].inc(64)
                both = [(projS[0], projS[0].n), (projS[1], projS[1].n)]
                for q in range(4):
                    chunk_stores.append(both)
            return (projS[0].n, projS[1].n)

        def emit_attention(layer, esc, s_q_done, lix, v_pre):
            t_entry = tms.n
            v_entry = vms.n
            kb = [esc.enter_context(nc.sbuf_tensor(f"kb{i}_{layer}", [128, 2048], BF16))
                  for i in range(2)]
            qsbs = [esc.enter_context(nc.sbuf_tensor(f"qsb{i}_{layer}", [128, LQ], BF16))
                    for i in range(2)]
            pt = esc.enter_context(nc.sbuf_tensor(f"pt_{layer}", [128, 4 * 1024], BF16))
            stg = esc.enter_context(nc.sbuf_tensor(f"stg_{layer}", [128, 2 * 512], BF16))
            if layer == 0:
                vts = [esc.enter_context(nc.sbuf_tensor("vt0", [128, NKT * 128], BF16))]
                masks_sb = esc.enter_context(nc.sbuf_tensor("masks_sb", [128, 16 * 512], BF16))
                qpd_sb = esc.enter_context(nc.sbuf_tensor("qpd_sb", [128, LQ], F32))
                dps = esc.enter_context(nc.sbuf_tensor("dps", [128, 2 * 512], F32))
                stg2 = None
            else:
                vts = [esc.enter_context(nc.sbuf_tensor(f"vaug{i}", [128, NKT * 130], BF16))
                       for i in range(2)]
                lrow = esc.enter_context(nc.sbuf_tensor("lrow", [1, 1024], F32))
                rlbc = esc.enter_context(nc.sbuf_tensor("rlbc", [128, 2 * 512], F32))
                stg2 = esc.enter_context(nc.sbuf_tensor("stg2_1", [128, 2 * 512], BF16))
            pAB = [esc.enter_context(nc.psum_tensor(f"pAB{i}_{layer}", [128, 1024]))
                   for i in range(3)]
            pC1 = esc.enter_context(nc.psum_tensor(f"pC1_{layer}", [128, 512]))
            if layer == 1:
                pC2 = esc.enter_context(nc.psum_tensor(f"pC2_{layer}", [128, 512]))

            v_ones = 0
            s_mask = 0
            v_lngpair = 0
            if layer == 0:
                def _(g, t_entry=t_entry, v_entry=v_entry, p0=projS[0].n, p1=projS[1].n):
                    g.wait_ge(tms.h, t_entry)
                    g.wait_ge(vms.h, v_entry)
                    g.wait_ge(projS[0].h, p0)
                    g.wait_ge(projS[1].h, p1)
                    g.dma_start(out=masks_sb[:].rearrange("p (m q) -> p m q", m=16),
                                 in_=masks_in.rearrange("m p q -> p m q")).then_inc(gMask.h, 16)
                p.on("gpsimd", _)
                g_mask = gMask.inc(16)

                def _(sy, t_entry=t_entry, v_entry=v_entry, p0=projS[0].n, p1=projS[1].n):
                    sy.wait_ge(tms.h, t_entry)
                    sy.wait_ge(vms.h, v_entry)
                    sy.wait_ge(projS[0].h, p0)
                    sy.wait_ge(projS[1].h, p1)
                    sy.dma_start(out=qpd_sb[:], in_=qpd_in).then_inc(sMask.h, 16)
                p.on("sync", _)
                s_mask = sMask.inc(16)

                def _(a):
                    a.wait_ge(aSmall.h, a_small)
                    a.activation(out=gam[:, 16:32], in_=gam[:, 0:16], func=AF.Ln).then_inc(ams.h, 1)
                p.on("scalar", _)
                a_lng = ams.inc()

                def _(te, a_lng=a_lng, t_entry=t_entry):
                    te.wait_ge(ams.h, a_lng)
                    te.wait_ge(sCst.h, s_const)
                    te.matmul(pAB[0][:, 0:8], sel_l, gam[0:1, 16:24], start=True, stop=False)
                    te.matmul(pAB[0][:, 0:8], sel_h, gam[0:1, 24:32], start=False, stop=True).then_inc(tms.h, 1)
                p.on("tensor", _)
                t_lgp = tms.inc()

                def _(v, t_lgp=t_lgp):
                    v.wait_ge(tms.h, t_lgp)
                    v.tensor_copy(out=lngpair[:], in_=pAB[0][:, 0:8]).then_inc(vms.h, 1)
                p.on("vector", _)
                v_lngpair = vms.inc()
            else:
                def _(v, t_entry=t_entry):
                    v.wait_ge(tms.h, t_entry)
                    for vb_ in vts:
                        va = vb_[:].rearrange("p (t g c) -> p t g c", t=NKT, g=2)
                        v.memset(va[:, :, :, 64:65], 1.0).then_inc(vms.h, 1)
                p.on("vector", _)
                vms.inc(len(vts) - 1)
                v_ones = vms.inc()

            q_loaded = [0] * NPAIR
            last_q_use = [0] * NPAIR
            nkload = [0]

            def load_q(p2):
                qwar = last_q_use[p2 - 1] if p2 >= 1 else 0

                def _(sy, p2=p2, qwar=qwar, t_entry=t_entry, v_entry=v_entry):
                    sy.wait_ge(ccK[lix][p2].h, 1)
                    if p2 == 0:
                        sy.wait_ge(projS[0].h, s_q_done[0])
                        sy.wait_ge(projS[1].h, s_q_done[1])
                        sy.wait_ge(tms.h, t_entry)
                        sy.wait_ge(vms.h, v_entry)
                    if qwar:
                        sy.wait_ge(tms.h, qwar)
                    sy.dma_start(out=qsbs[p2 % 2][:], in_=qTd[128 * p2:128 * (p2 + 1), :]).then_inc(qS[p2 % 2].h, 16)
                p.on("sync", _)
                q_loaded[p2] = (qS[p2 % 2], qS[p2 % 2].inc(16))

            def load_kband(p2, w, war):
                par = nkload[0] % 2
                nkload[0] += 1

                def _(sy, p2=p2, w=w, war=war, par=par):
                    sy.wait_ge(ccK[lix][p2].h, 1)
                    if war:
                        sy.wait_ge(tms.h, war)
                    else:
                        sy.wait_ge(projS[0].h, s_q_done[0])
                        sy.wait_ge(projS[1].h, s_q_done[1])
                    for r in range(4):
                        sy.dma_start(out=kb[par][:, 512 * r:512 * (r + 1)],
                                     in_=kg8[p2][128 * r:128 * (r + 1),
                                                 512 * w:512 * (w + 1)]).then_inc(kbS[par].h, 16)
                p.on("sync", _)
                return par, (kbS[par], kbS[par].inc(64))

            load_q(0)
            stg_war = [0, 0]
            epil_done = 0
            kb_use = []
            ptev = []
            pv_done = []

            def emit_vwindow(p2, w, pv_guard):
                vpar = (p2 % 2) if layer == 1 else 0
                sem = vtS[w + 4 * vpar]

                def _(sy, p2=p2, w=w, pv_guard=pv_guard, vpar=vpar, sem=sem,
                      vo=(v_ones if layer == 1 else 0)):
                    sy.wait_ge(ccV[lix][p2].h, 1)
                    if pv_guard:
                        sy.wait_ge(tms.h, pv_guard)
                    else:
                        sy.wait_ge(projS[0].h, s_q_done[0])
                        sy.wait_ge(projS[1].h, s_q_done[1])
                    if vo:
                        sy.wait_ge(vms.h, vo)
                    for r in range(4):
                        srcw = vgT8[p2, r, 512 * w:512 * (w + 1), :]
                        if layer == 0:
                            dstw = vts[0][:].rearrange("p (lt r e) -> p lt r e",
                                                       r=4, e=128)[:, 4 * w:4 * (w + 1), r, :]
                            sy.dma_start(out=dstw,
                                         in_=srcw.rearrange("(i p) e -> p i e",
                                                            p=128)).then_inc(sem.h, 16)
                        else:
                            for gg in range(2):
                                dstw = vts[vpar][:].rearrange(
                                    "p (lt r g e) -> p lt r g e",
                                    r=4, g=2, e=65)[:, 4 * w:4 * (w + 1), r, gg, 0:64]
                                sy.dma_start(
                                    out=dstw,
                                    in_=srcw.rearrange("(i p) (g e) -> p i g e",
                                                       p=128, g=2)[:, :, gg, :]).then_inc(sem.h, 16)
                p.on("sync", _)
                return (sem, sem.inc(64 if layer == 0 else 128))

            wins = {}
            for p2 in range(NPAIR):
                if p2 + 1 < NPAIR:
                    load_q(p2 + 1)

                pv_guard = pv_done[-1] if pv_done else 0
                if layer == 0:
                    wins[p2] = {w: emit_vwindow(p2, w, pv_guard) for w in range(4)}
                else:
                    if p2 == 0:
                        wins[0] = {w: emit_vwindow(0, w, 0) for w in range(4)}
                    if p2 + 1 < NPAIR:
                        wins[p2 + 1] = {w: emit_vwindow(p2 + 1, w, pv_guard)
                                        for w in range(4)}
                win_done = wins[p2]
                for qc in range(4):
                    windows = list(range(qc + 1)) if layer == 0 else [0, 1, 2, 3]

                    first_j = True
                    qc_base = len(ptev)
                    pend = []

                    vpar_pv = (p2 % 2) if layer == 1 else 0

                    def emit_pv(rec, epw, vpar_pv=vpar_pv):
                        idx, j, ppar, trj, first, last = rec
                        ev_list = ptev[idx]
                        vt = vts[vpar_pv]

                        def _(te, j=j, ppar=ppar, ev_list=ev_list, trj=trj, vt=vt,
                              first=first, last=last, epw=epw, layer=layer):
                            for _s, _v in ev_list:
                                te.wait_ge(_s.h, _v)
                            te.wait_ge(trj[0].h, trj[1])
                            if epw:
                                te.wait_ge(vms.h, epw)
                            h1 = pt[:, 1024 * ppar:1024 * ppar + 512]
                            h2 = pt[:, 1024 * ppar + 512:1024 * (ppar + 1)]
                            if layer == 0:
                                te.matmul(pC1[0:64, :], vt[:, 128 * j:128 * j + 64],
                                          h1, start=first, stop=last,
                                          tile_position=(0, 0), skip_group_check=True)
                                ins = te.matmul(pC1[64:128, :],
                                                vt[:, 128 * j + 64:128 * (j + 1)],
                                                h2, start=first, stop=last,
                                                tile_position=(0, 64),
                                                skip_group_check=True)
                            else:
                                te.matmul(pC1[0:65, :], vt[:, 130 * j:130 * j + 65],
                                          h1, start=first, stop=last)
                                ins = te.matmul(pC2[0:65, :],
                                                vt[:, 130 * j + 65:130 * (j + 1)],
                                                h2, start=first, stop=last)
                            ins.then_inc(tms.h, 1)
                        p.on("tensor", _)
                        pv_done.append(tms.inc())

                    for w in windows:
                        kwar = kb_use[-2] if len(kb_use) >= 2 else 0
                        kpar, k_ld = load_kband(p2, w, kwar)
                        for jj in range(16):
                            j = 16 * w + jj
                            idx = len(ptev)
                            par = idx % 3
                            ppar = idx % 4
                            warp = ptev[idx - 3] if idx >= 3 else None
                            ep_war = epil_done if (first_j and layer == 1) else 0
                            vlp = (v_lngpair if (layer == 0 and p2 == 0 and qc == 0
                                                 and idx == 0) else 0)
                            kcol = 512 * (j % 4) + 128 * ((j // 4) % 4)

                            def _(te, p2=p2, qc=qc, idx=idx, par=par, warp=warp,
                                  ep_war=ep_war, vlp=vlp, kpar=kpar,
                                  k_ld=k_ld, kcol=kcol, fj=first_j):
                                if fj:
                                    ql = q_loaded[p2]
                                    te.wait_ge(ql[0].h, ql[1])
                                te.wait_ge(k_ld[0].h, k_ld[1])
                                if warp:
                                    for _s, _v in warp:
                                        te.wait_ge(_s.h, _v)
                                if ep_war:
                                    te.wait_ge(vms.h, ep_war)
                                if vlp:
                                    te.wait_ge(vms.h, vlp)
                                te.matmul(pAB[par][:, 0:512],
                                          kb[kpar][0:64, kcol:kcol + 128],
                                          qsbs[p2 % 2][0:64, 512 * qc:512 * (qc + 1)],
                                          start=True, stop=True)
                                te.matmul(pAB[par][:, 512:1024],
                                          kb[kpar][64:128, kcol:kcol + 128],
                                          qsbs[p2 % 2][64:128, 512 * qc:512 * (qc + 1)],
                                          start=True, stop=True,
                                          tile_position=(64, 0)).then_inc(tms.h, 1)
                            p.on("tensor", _)
                            first_j = False
                            t_qk = tms.inc()
                            last_q_use[p2] = t_qk

                            pvw = pv_done[idx - 4] if idx >= 4 else 0
                            if layer == 0 and w == qc:
                                mcol = 512 * jj

                                def _(v, par=par, ppar=ppar, t_qk=t_qk, mcol=mcol,
                                      pvw=pvw, idx=idx):
                                    v.wait_ge(tms.h, t_qk)
                                    if pvw:
                                        v.wait_ge(tms.h, pvw)
                                    if idx == 0:
                                        v.wait_ge(gMask.h, g_mask)
                                    v.tensor_tensor(out=pt[:, 1024 * ppar:1024 * ppar + 512],
                                                    in0=pAB[par][:, 0:512],
                                                    in1=masks_sb[:, mcol:mcol + 512],
                                                    op=ALU.mult)
                                    v.tensor_tensor(out=pt[:, 1024 * ppar + 512:1024 * (ppar + 1)],
                                                    in0=pAB[par][:, 512:1024],
                                                    in1=masks_sb[:, mcol:mcol + 512],
                                                    op=ALU.mult).then_inc(vms.h, 1)
                                p.on("vector", _)
                                ptev.append([(vms, vms.inc())])
                            elif layer == 1:
                                def _(a, par=par, ppar=ppar, t_qk=t_qk, pvw=pvw, idx=idx):
                                    a.wait_ge(tms.h, t_qk)
                                    if pvw:
                                        a.wait_ge(tms.h, pvw)
                                    if idx == 0:
                                        a.wait_ge(vms.h, v_cst)
                                    a.activation(out=pt[:, 1024 * ppar:1024 * (ppar + 1)],
                                                 in_=pAB[par][:], func=AF.Exp,
                                                 scale=cst[:, 1:2]).then_inc(ams.h, 1)
                                p.on("scalar", _)
                                ptev.append([(ams, ams.inc())])
                            else:
                                def _(v, par=par, ppar=ppar, t_qk=t_qk, pvw=pvw):
                                    v.wait_ge(tms.h, t_qk)
                                    if pvw:
                                        v.wait_ge(tms.h, pvw)
                                    v.tensor_copy(out=pt[:, 1024 * ppar:1024 * ppar + 512],
                                                  in_=pAB[par][:, 0:512]).then_inc(vms.h, 1)
                                p.on("vector", _)

                                def _(a, par=par, ppar=ppar, t_qk=t_qk, pvw=pvw):
                                    a.wait_ge(tms.h, t_qk)
                                    if pvw:
                                        a.wait_ge(tms.h, pvw)
                                    a.activation(out=pt[:, 1024 * ppar + 512:1024 * (ppar + 1)],
                                                 in_=pAB[par][:, 512:1024],
                                                 func=AF.Identity).then_inc(ams.h, 1)
                                p.on("scalar", _)
                                ptev.append([(vms, vms.inc()), (ams, ams.inc())])

                            pend.append((idx, j, ppar, win_done[w],
                                         idx == qc_base,
                                         w == windows[-1] and jj == 15))
                            if len(pend) > 3:
                                rec = pend.pop(0)
                                emit_pv(rec, epil_done if rec[4] else 0)
                        kb_use.append(tms.n)
                    while pend:
                        rec = pend.pop(0)
                        emit_pv(rec, epil_done if rec[4] else 0)

                    t_pv_last = pv_done[-1]
                    slot = (p2 * 4 + qc) % 2
                    if layer == 0:
                        dpw = epil_done

                        def _(a, p2=p2, qc=qc, dpw=dpw, v_lngpair=v_lngpair):
                            a.wait_ge(vms.h, v_lngpair)
                            a.wait_ge(sMask.h, s_mask)
                            if dpw:
                                a.wait_ge(vms.h, dpw)
                            a.activation(out=dps[:, 512 * (qc % 2):512 * (qc % 2 + 1)],
                                         in_=qpd_sb[:, 512 * qc:512 * (qc + 1)],
                                         func=AF.Exp, scale=lngpair[:, p2:p2 + 1]).then_inc(ams.h, 1)
                        p.on("scalar", _)
                        a_dp = ams.inc()

                        def _(v, qc=qc, slot=slot, a_dp=a_dp, t_pv_last=t_pv_last,
                              war=stg_war[slot]):
                            v.wait_ge(ams.h, a_dp)
                            v.wait_ge(tms.h, t_pv_last)
                            if war:
                                v.wait_ge(stgS[slot].h, war)
                            v.tensor_tensor(out=stg[:, 512 * slot:512 * (slot + 1)],
                                            in0=pC1[:],
                                            in1=dps[:, 512 * (qc % 2):512 * (qc % 2 + 1)],
                                            op=ALU.mult).then_inc(vms.h, 1)
                        p.on("vector", _)
                        epil_done = vms.inc()

                        def _(sy, p2=p2, qc=qc, slot=slot, v_stg=epil_done):
                            sy.wait_ge(vms.h, v_stg)
                            sy.dma_start(out=aoTd[128 * p2:128 * (p2 + 1), 512 * qc:512 * (qc + 1)],
                                         in_=stg[:, 512 * slot:512 * (slot + 1)]).then_inc(stgS[slot].h, 16)
                        p.on("sync", _)
                        stg_war[slot] = stgS[slot].inc(16)
                    else:
                        base_lr = vms.n

                        def _(v, t_pv_last=t_pv_last, base_lr=base_lr):
                            v.wait_ge(tms.h, t_pv_last)
                            v.tensor_copy(out=lrow[0:1, 0:512], in_=pC1[64:65, :]).then_inc(vms.h, 1)
                            v.tensor_copy(out=lrow[0:1, 512:1024], in_=pC2[64:65, :]).then_inc(vms.h, 1)
                            v.wait_ge(vms.h, base_lr + 2)
                            v.reciprocal(out=lrow[:], in_=lrow[:]).then_inc(vms.h, 1)
                        p.on("vector", _)
                        vms.inc(2)
                        v_lr = vms.inc()

                        def _(te, v_lr=v_lr):
                            te.wait_ge(vms.h, v_lr)
                            te.wait_ge(sCst.h, s_const)
                            te.matmul(pAB[0][:, 0:512], sel_l, lrow[0:1, 0:512], start=True, stop=True)
                            te.matmul(pAB[0][:, 512:1024], sel_l, lrow[0:1, 512:1024],
                                      start=True, stop=True).then_inc(tms.h, 1)
                        p.on("tensor", _)
                        t_rl = tms.inc()
                        last_q_use[p2] = max(last_q_use[p2], t_rl)

                        base_rl = vms.n

                        def _(v, slot=slot, t_rl=t_rl, war=stg_war[slot], base_rl=base_rl):
                            v.wait_ge(tms.h, t_rl)
                            v.tensor_copy(out=rlbc[:, 0:512], in_=pAB[0][:, 0:512]).then_inc(vms.h, 1)
                            v.tensor_copy(out=rlbc[:, 512:1024], in_=pAB[0][:, 512:1024]).then_inc(vms.h, 1)
                            if war:
                                v.wait_ge(stgS[slot].h, war)
                            v.wait_ge(vms.h, base_rl + 2)
                            v.tensor_tensor(out=stg[0:64, 512 * slot:512 * (slot + 1)],
                                            in0=pC1[0:64, :], in1=rlbc[0:64, 0:512],
                                            op=ALU.mult)
                            v.tensor_tensor(out=stg2[0:64, 512 * slot:512 * (slot + 1)],
                                            in0=pC2[0:64, :], in1=rlbc[0:64, 512:1024],
                                            op=ALU.mult).then_inc(vms.h, 1)
                        p.on("vector", _)
                        vms.inc(2)
                        epil_done = vms.inc()

                        def _(sy, p2=p2, qc=qc, slot=slot, v_stg=epil_done):
                            sy.wait_ge(vms.h, v_stg)
                            sy.dma_start(out=aoTd[128 * p2:128 * p2 + 64, 512 * qc:512 * (qc + 1)],
                                         in_=stg[0:64, 512 * slot:512 * (slot + 1)]).then_inc(stgS[slot].h, 16)
                            sy.dma_start(out=aoTd[128 * p2 + 64:128 * (p2 + 1), 512 * qc:512 * (qc + 1)],
                                         in_=stg2[0:64, 512 * slot:512 * (slot + 1)]).then_inc(stgS[slot].h, 16)
                        p.on("sync", _)
                        stg_war[slot] = stgS[slot].inc(32)
            return (stgS[0].n, stgS[1].n)

        def emit_oproj(esc, wo_sb, s_ao, w_sem, w_val, layer):
            aot = esc.enter_context(nc.sbuf_tensor(f"aot_{layer}", [128, 4 * LQ], BF16))
            tmp = esc.enter_context(nc.sbuf_tensor(f"otmp_{layer}", [128, 1024], F32))
            bbc = esc.enter_context(nc.sbuf_tensor(f"bbc_{layer}", [128, 1024], F32))
            brow_o = esc.enter_context(nc.sbuf_tensor(f"brow_{layer}", [1, D], F32))
            pOs = [[esc.enter_context(nc.psum_tensor(f"pO{2*i+n}_{layer}", [128, 512]))
                    for n in range(2)] for i in range(2)]
            pBB = esc.enter_context(nc.psum_tensor(f"pBB_{layer}", [128, 512]))
            tg, vg_ = tms.n, vms.n

            def _(a, tg=tg, vg_=vg_, s_ao=s_ao):
                a.wait_ge(tms.h, tg)
                a.wait_ge(vms.h, vg_)
                a.wait_ge(stgS[0].h, s_ao[0])
                a.wait_ge(stgS[1].h, s_ao[1])
                sbo = aBo0 if layer == 0 else aBo1
                a.dma_start(out=brow_o[:], in_=(bo0 if layer == 0 else bo1)).then_inc(sbo.h, 16)
            p.on("scalar", _)
            a_bo = ((aBo0 if layer == 0 else aBo1), (aBo0 if layer == 0 else aBo1).inc(16))

            v_bbc = 0
            for half_b in range(2):
                def _(te, half_b=half_b, v_bbc=v_bbc, a_bo=a_bo, s_ao=s_ao):
                    te.wait_ge(a_bo[0].h, a_bo[1])
                    te.wait_ge(stgS[0].h, s_ao[0])
                    te.wait_ge(stgS[1].h, s_ao[1])
                    te.wait_ge(sCst.h, s_const)
                    if v_bbc:
                        te.wait_ge(vms.h, v_bbc)
                    te.matmul(pBB[:], ones_r,
                              brow_o[0:1, 512 * half_b:512 * (half_b + 1)],
                              start=True, stop=True).then_inc(tms.h, 1)
                p.on("tensor", _)
                t_b = tms.inc()

                def _(v, half_b=half_b, t_b=t_b):
                    v.wait_ge(tms.h, t_b)
                    v.tensor_copy(out=bbc[:, 512 * half_b:512 * (half_b + 1)],
                                  in_=pBB[:]).then_inc(vms.h, 1)
                p.on("vector", _)
                v_bbc = vms.inc()

            ep_done = []
            t_half_last = 0
            for half in range(2):
                def _(sy, half=half, s_ao=s_ao, t_half_last=t_half_last):
                    sy.wait_ge(stgS[0].h, s_ao[0])
                    sy.wait_ge(stgS[1].h, s_ao[1])
                    if t_half_last:
                        sy.wait_ge(tms.h, t_half_last)
                    for cl in range(4):
                        c2 = 4 * half + cl
                        sy.dma_start(out=aot[:, LQ * cl:LQ * (cl + 1)],
                                     in_=aoTd[128 * c2:128 * (c2 + 1), :]).then_inc(aotS.h, 16)
                p.on("sync", _)
                s_load = aotS.inc(64)
                for tt in range(NT):
                    idx = half * NT + tt
                    banks = pOs[idx % 2]
                    warv = ep_done[idx - 2] if idx >= 2 else 0

                    def _(te, half=half, tt=tt, idx=idx, banks=banks, s_load=s_load,
                          warv=warv):
                        if idx == 0:
                            te.wait_ge(w_sem.h, w_val)
                        te.wait_ge(aotS.h, s_load)
                        if warv:
                            te.wait_ge(vms.h, warv)
                        for n in range(2):
                            for cl in range(4):
                                ins = te.matmul(
                                    banks[n][:],
                                    aot[:, LQ * cl + 128 * tt:LQ * cl + 128 * (tt + 1)],
                                    wo_sb[:, D * (4 * half + cl) + 512 * n:
                                          D * (4 * half + cl) + 512 * (n + 1)],
                                    start=(cl == 0), stop=(cl == 3))
                        ins.then_inc(tms.h, 1)
                    p.on("tensor", _)
                    t_o = tms.inc()
                    t_half_last = t_o

                    ep_base = vms.n
                    prev_ep = ep_done[-1] if ep_done else 0

                    def _(v, half=half, tt=tt, banks=banks, t_o=t_o, v_bbc=v_bbc,
                          ep_base=ep_base, prev_ep=prev_ep):
                        v.wait_ge(tms.h, t_o)
                        if half == 1 and prev_ep:
                            v.wait_ge(vms.h, prev_ep)
                        if half == 0:
                            v.tensor_add(out=xs[:, 1024 * tt:1024 * tt + 512],
                                         in0=xs[:, 1024 * tt:1024 * tt + 512], in1=banks[0][:])
                            v.tensor_add(out=xs[:, 1024 * tt + 512:1024 * (tt + 1)],
                                         in0=xs[:, 1024 * tt + 512:1024 * (tt + 1)],
                                         in1=banks[1][:]).then_inc(vms.h, 1)
                        else:
                            v.wait_ge(vms.h, v_bbc)
                            v.tensor_add(out=tmp[:, 0:512], in0=banks[0][:], in1=bbc[:, 0:512])
                            v.tensor_add(out=tmp[:, 512:1024], in0=banks[1][:],
                                         in1=bbc[:, 512:1024]).then_inc(vms.h, 1)
                            v.wait_ge(vms.h, ep_base + 1)
                            v.tensor_add(out=xs[:, 1024 * tt:1024 * tt + 512],
                                         in0=xs[:, 1024 * tt:1024 * tt + 512], in1=tmp[:, 0:512])
                            v.tensor_add(out=xs[:, 1024 * tt + 512:1024 * (tt + 1)],
                                         in0=xs[:, 1024 * tt + 512:1024 * (tt + 1)],
                                         in1=tmp[:, 512:1024]).then_inc(vms.h, 1)
                    p.on("vector", _)
                    if half == 1:
                        vms.inc(1)
                    ep_done.append(vms.inc())
            return vms.n

        def emit_ffn(esc, v_lnT):
            hbuf = esc.enter_context(nc.sbuf_tensor("hbuf", [128, 32 * 512], BF16))
            gbuf = esc.enter_context(nc.sbuf_tensor("gbuf", [128, 32 * 512], BF16))
            wbs = [esc.enter_context(nc.sbuf_tensor(f"wb{i}", [128, 8 * 512], BF16))
                   for i in range(2)]
            w3sb = [esc.enter_context(nc.sbuf_tensor(f"w3s{i}", [128, 1024], BF16))
                    for i in range(2)]
            ftmp = esc.enter_context(nc.sbuf_tensor("ftmp", [128, 1024], F32))
            scl = esc.enter_context(nc.sbuf_tensor("scl", [128, 4 * 32], F32))
            s5row = esc.enter_context(nc.sbuf_tensor("s5row", [1, D], F32))
            bbc = esc.enter_context(nc.sbuf_tensor("bbc_f", [128, 1024], F32))
            pb = [esc.enter_context(nc.psum_tensor(f"pfb{i}", [128, 512])) for i in range(8)]
            tg, vg_ = tms.n, vms.n

            def _(a, tg=tg, vg_=vg_):
                a.wait_ge(tms.h, tg)
                a.wait_ge(vms.h, vg_)
                for dst, srcw in [(scl[:, 0:32], s1_in), (scl[:, 32:64], s2_in),
                                  (scl[:, 64:96], s3_in), (scl[:, 96:128], s4_in),
                                  (s5row[:], s5_in)]:
                    a.dma_start(out=dst, in_=srcw).then_inc(aScl.h, 16)
            p.on("scalar", _)
            a_scl = aScl.inc(16 * 5)

            v_w3sbc = 0
            for hb in range(2):
                def _(te, hb=hb, v_w3sbc=v_w3sbc, a_scl=a_scl):
                    te.wait_ge(aScl.h, a_scl)
                    te.wait_ge(sCst.h, s_const)
                    if v_w3sbc:
                        te.wait_ge(vms.h, v_w3sbc)
                    te.matmul(pb[6][:], ones_r, s5row[:, 512 * hb:512 * (hb + 1)],
                              start=True, stop=True).then_inc(tms.h, 1)
                p.on("tensor", _)
                t_w = tms.inc()

                def _(v, hb=hb, t_w=t_w):
                    v.wait_ge(tms.h, t_w)
                    v.tensor_copy(out=bbc[:, 512 * hb:512 * (hb + 1)],
                                  in_=pb[6][:]).then_inc(vms.h, 1)
                p.on("vector", _)
                v_w3sbc = vms.inc()

            nwl = [0]
            wload = []

            def stream_block(src_ap, war_t):
                par = nwl[0] % 2
                nwl[0] += 1

                def _(sy, src_ap=src_ap, war_t=war_t, par=par):
                    sy.wait_ge(sCache.h, g_cache)
                    if war_t:
                        sy.wait_ge(tms.h, war_t)
                    sy.dma_start(out=wbs[par][:].rearrange("p (ic o) -> p ic o", ic=8),
                                 in_=src_ap.rearrange("(ic p) o -> p ic o", p=128)).then_inc(wbsS[par].h, 16)
                p.on("sync", _)
                wload.append(wbsS[par].inc(16))
                return par, wload[-1]

            blk_mm = []
            ep3_prev = 0
            for u in range(4):
                ev_h12 = []
                for wi, (cache, sc_off, fn, dst) in enumerate(
                        [(w1c, 0, AF.Silu, hbuf), (w2c, 32, AF.Identity, gbuf)]):
                    for og in range(8):
                        bi = len(blk_mm)
                        par, s_w = stream_block(cache[og], blk_mm[bi - 2] if bi >= 2 else 0)
                        warv = ev_h12[-1] if len(ev_h12) >= 1 else 0

                        def _(te, par=par, s_w=s_w, og=og, wi=wi, u=u, warv=warv):
                            te.wait_ge(wbsS[par].h, s_w)
                            if og == 0 and wi == 0 and u == 0:
                                te.wait_ge(vms.h, v_lnT)
                            if warv:
                                te.wait_ge(ams.h, warv)
                            for oc in range(4):
                                for ic in range(8):
                                    ins = te.matmul(
                                        pb[oc][:],
                                        wbs[par][:, 512 * ic + 128 * oc:512 * ic + 128 * (oc + 1)],
                                        lnT[:, LQ * ic + 512 * u:LQ * ic + 512 * (u + 1)],
                                        start=(ic == 0), stop=(ic == 7))
                                ins.then_inc(tms.h, 1)
                        p.on("tensor", _)
                        for oc in range(4):
                            tms.inc()
                        blk_mm.append(tms.n)
                        t_base = tms.n - 4

                        def _(a, par=par, og=og, dst=dst, sc_off=sc_off, fn=fn, t_base=t_base,
                              a_scl=a_scl):
                            a.wait_ge(aScl.h, a_scl)
                            for oc in range(4):
                                a.wait_ge(tms.h, t_base + oc + 1)
                                ocp = 4 * og + oc
                                ins = a.activation(out=dst[:, 512 * ocp:512 * (ocp + 1)],
                                                   in_=pb[oc][:], func=fn,
                                                   scale=scl[:, sc_off + ocp:sc_off + ocp + 1])
                            ins.then_inc(ams.h, 1)
                        p.on("scalar", _)
                        ev_h12.append(ams.inc())

                a_w12 = ams.n

                def _(v, a_w12=a_w12):
                    v.wait_ge(ams.h, a_w12)
                    v.tensor_mul(out=hbuf[:], in0=hbuf[:], in1=gbuf[:]).then_inc(vms.h, 1)
                p.on("vector", _)
                v_h = vms.inc()

                prev_stage_a = 0
                for hi, (cache, sc_off, src, dst) in enumerate(
                        [(h1c, 64, hbuf, gbuf), (h2c, 96, gbuf, hbuf)]):
                    ev_hh = []
                    for og in range(8):
                        for icg in range(4):
                            bi = len(blk_mm)
                            par, s_w = stream_block(cache[og][1024 * icg:1024 * (icg + 1), :],
                                                    blk_mm[bi - 2] if bi >= 2 else 0)
                            warv = (ev_hh[-1] if ev_hh else (ev_h12[-1] if hi == 0 else 0))
                            dep_v = v_h if hi == 0 else 0
                            dep_a = prev_stage_a if hi == 1 else 0

                            def _(te, par=par, s_w=s_w, icg=icg, og=og, src=src,
                                  dep_v=dep_v, dep_a=dep_a, warv=warv):
                                te.wait_ge(wbsS[par].h, s_w)
                                if icg == 0 and og == 0:
                                    if dep_v:
                                        te.wait_ge(vms.h, dep_v)
                                    if dep_a:
                                        te.wait_ge(ams.h, dep_a)
                                if icg == 0 and warv:
                                    te.wait_ge(ams.h, warv)
                                for oc in range(4):
                                    for ic8 in range(8):
                                        ic = icg * 8 + ic8
                                        ins = te.matmul(
                                            pb[oc][:],
                                            wbs[par][:, 512 * ic8 + 128 * oc:512 * ic8 + 128 * (oc + 1)],
                                            src[:, 512 * ic:512 * (ic + 1)],
                                            start=(icg == 0 and ic8 == 0),
                                            stop=(icg == 3 and ic8 == 7),
                                            skip_group_check=True)
                                    ins.then_inc(tms.h, 1)
                            p.on("tensor", _)
                            for oc in range(4):
                                tms.inc()
                            blk_mm.append(tms.n)
                        t_base = tms.n - 4

                        def _(a, og=og, dst=dst, sc_off=sc_off, t_base=t_base):
                            for oc in range(4):
                                a.wait_ge(tms.h, t_base + oc + 1)
                                ocp = 4 * og + oc
                                ins = a.activation(out=dst[:, 512 * ocp:512 * (ocp + 1)],
                                                   in_=pb[oc][:], func=AF.Gelu,
                                                   scale=scl[:, sc_off + ocp:sc_off + ocp + 1])
                            ins.then_inc(ams.h, 1)
                        p.on("scalar", _)
                        ev_hh.append(ams.inc())
                    prev_stage_a = ams.n

                a_h2 = ams.n

                w3mm = []
                for th in range(2):
                    ep_guard = ep3_prev
                    for ic32 in range(32):
                        par = ic32 % 2
                        war_w3 = w3mm[-2] if len(w3mm) >= 2 else 0

                        def _(sy, par=par, ic32=ic32, war_w3=war_w3):
                            sy.wait_ge(sCache.h, g_cache)
                            if war_w3:
                                sy.wait_ge(tms.h, war_w3)
                            sy.dma_start(out=w3sb[par][:],
                                         in_=w3c[128 * ic32:128 * (ic32 + 1), :]).then_inc(w3S[par].h, 16)
                        p.on("sync", _)
                        s_wv = w3S[par].inc(16)

                        def _(te, par=par, ic32=ic32, th=th, a_h2=a_h2, s_wv=s_wv,
                              ep_guard=ep_guard):
                            te.wait_ge(w3S[par].h, s_wv)
                            if ic32 == 0:
                                te.wait_ge(ams.h, a_h2)
                                if ep_guard:
                                    te.wait_ge(vms.h, ep_guard)
                            for ti in range(2):
                                tt = th * 2 + ti
                                for n in range(2):
                                    ins = te.matmul(
                                        pb[4 + 2 * ti + n][:],
                                        hbuf[:, 512 * ic32 + 128 * tt:512 * ic32 + 128 * (tt + 1)],
                                        w3sb[par][:, 512 * n:512 * (n + 1)],
                                        start=(ic32 == 0), stop=(ic32 == 31),
                                        skip_group_check=True)
                            ins.then_inc(tms.h, 1)
                        p.on("tensor", _)
                        w3mm.append(tms.inc())
                    t_w3f = w3mm[-1]

                    base_w3 = vms.n

                    def _(v, u=u, th=th, t_w3f=t_w3f, v_w3sbc=v_w3sbc, base_w3=base_w3):
                        v.wait_ge(tms.h, t_w3f)
                        v.wait_ge(vms.h, v_w3sbc)
                        b = base_w3
                        for ti in range(2):
                            gt = 4 * u + th * 2 + ti
                            v.tensor_tensor(out=ftmp[:, 0:512], in0=pb[4 + 2 * ti][:],
                                            in1=bbc[:, 0:512], op=ALU.mult)
                            v.tensor_tensor(out=ftmp[:, 512:1024], in0=pb[4 + 2 * ti + 1][:],
                                            in1=bbc[:, 512:1024], op=ALU.mult).then_inc(vms.h, 1)
                            b += 1
                            v.wait_ge(vms.h, b)
                            v.tensor_add(out=xs[:, 1024 * gt:1024 * gt + 512],
                                         in0=xs[:, 1024 * gt:1024 * gt + 512], in1=ftmp[:, 0:512])
                            v.tensor_add(out=xs[:, 1024 * gt + 512:1024 * (gt + 1)],
                                         in0=xs[:, 1024 * gt + 512:1024 * (gt + 1)],
                                         in1=ftmp[:, 512:1024]).then_inc(vms.h, 1)
                            b += 1
                            if ti == 0:
                                v.wait_ge(vms.h, b)
                    p.on("vector", _)
                    vms.inc(3)
                    ep3_prev = vms.inc()
            return vms.n

        eW0 = es.enter_context(contextlib.ExitStack())
        wO = eW0.enter_context(nc.sbuf_tensor("wO", [128, NCD * D], BF16))
        with contextlib.ExitStack() as e1:
            wA = e1.enter_context(nc.sbuf_tensor("wA", [128, NCD * D], BF16))
            wB = e1.enter_context(nc.sbuf_tensor("wB", [128, NCD * D], BF16))
            wC = e1.enter_context(nc.sbuf_tensor("wC", [128, NCD * D], BF16))
            stage = e1.enter_context(nc.sbuf_tensor("stage0", [128, 2 * LQ], BF16))

            def _(g):
                for dst, srcw in [(wA, wqT0), (wB, wkT0), (wC, wvT0), (wO, woT0)]:
                    g.dma_start(out=dst[:].rearrange("p (c o) -> p c o", c=NCD),
                                in_=srcw.rearrange("(c p) o -> p c o", p=128)).then_inc(gW0.h, 16)
            p.on("gpsimd", _)
            g_wqkv0 = gW0.inc(64)
            g_wo0 = g_wqkv0

            if upto >= 3.9:
                def _(g):
                    for og in range(8):
                        g.dma_start(out=w1c[og], in_=w1b[og]).then_inc(sCache.h, 16)
                        g.dma_start(out=w2c[og], in_=w2b[og]).then_inc(sCache.h, 16)
                    for og in range(8):
                        g.dma_start(out=h1c[og], in_=h1b[og]).then_inc(sCache.h, 16)
                        g.dma_start(out=h2c[og], in_=h2b[og]).then_inc(sCache.h, 16)
                    for q in range(4):
                        g.dma_start(out=w3c[1024 * q:1024 * (q + 1), :],
                                    in_=w3T_in[1024 * q:1024 * (q + 1), :]).then_inc(sCache.h, 16)
                p.on("gpsimd", _)
            g_cache = sCache.inc(16 * 36) if upto >= 3.9 else sCache.n

            s_vin0_box = [0]
            v_ln1 = emit_ln("ln1", e1)
            pps = [e1.enter_context(nc.psum_tensor(f"pp{i}", [128, 512])) for i in range(2)]

            def _(te):
                te.wait_ge(sCst.h, s_const)
                te.wait_ge(aSmall.h, a_small)
                te.matmul(pps[0][:], ones_r, bvrow_sb[0:1, 0:512], start=True, stop=True)
                te.matmul(pps[1][:], ones_r, bvrow_sb[0:1, 512:1024],
                          start=True, stop=True).then_inc(tms.h, 1)
            p.on("tensor", _)
            t_bv = tms.inc()

            def _(v, t_bv=t_bv):
                v.wait_ge(tms.h, t_bv)
                v.tensor_copy(out=bvbc[:, 0:512], in_=pps[0][:])
                v.tensor_copy(out=bvbc[:, 512:1024], in_=pps[1][:]).then_inc(vms.h, 1)
            p.on("vector", _)
            proj_ev.append(vms.inc())
            proj_ev.append(proj_ev[-1])

            kst0, vst0 = [], []
            emit_proj(wA, qTd, biasq[:, 0:NCD], pps, v_ln1, gW0, g_wqkv0, stage)
            emit_proj(wB, kin, biasq[:, NCD:2 * NCD], pps, v_ln1, gW0, g_wqkv0, stage,
                      chunk_stores=kst0)
            s_vin0 = emit_projT(wC, bvbc, pps, v_ln1, gW0, g_wqkv0, stage,
                                chunk_stores=vst0)
            s_vin0_box[0] = s_vin0

        if upto >= 2:
            def _(g, kst0=kst0, vst0=vst0):
                for c in range(8):
                    g.wait_ge(kst0[c][0].h, kst0[c][1])
                    g.collective_compute("AllGather", ALU.bypass,
                                         replica_groups=[[0, 1, 2, 3], [4, 5, 6, 7]],
                                         ins=[kin[128 * c:128 * (c + 1), :]],
                                         outs=[kg8[c]]).then_inc(ccK[0][c].h)
                for c in range(8):
                    for sv in vst0[c]:
                        g.wait_ge(sv[0].h, sv[1])
                    g.collective_compute("AllGather", ALU.bypass,
                                         replica_groups=[[0, 1, 2, 3], [4, 5, 6, 7]],
                                         ins=[vinT[c]],
                                         outs=[vgT8[c]]).then_inc(ccV[0][c].h)
            p.on("gpsimd", _)

            if upto >= 2.5:
                with contextlib.ExitStack() as e2:
                    s_ao0 = emit_attention(0, e2, s_vin0, 0, 0)
        if upto >= 3:
            with contextlib.ExitStack() as e3:
                v_x1 = emit_oproj(e3, wO, s_ao0, gW0, g_wo0, 0)
        eW0.close()
        if upto >= 4:
            with contextlib.ExitStack() as e4:
                v_ln2 = emit_ln("ln2", e4)
            with contextlib.ExitStack() as e5:
                v_x2 = emit_ffn(e5, v_ln2)

        if upto >= 5:
            t_l0_done = tms.n
            v_l0_done = vms.n
            eW1 = es.enter_context(contextlib.ExitStack())
            wO = eW1.enter_context(nc.sbuf_tensor("wO1", [128, NCD * D], BF16))
            with contextlib.ExitStack() as e6:
                wA = e6.enter_context(nc.sbuf_tensor("wA1", [128, NCD * D], BF16))
                wB = e6.enter_context(nc.sbuf_tensor("wB1", [128, NCD * D], BF16))
                wC = e6.enter_context(nc.sbuf_tensor("wC1", [128, NCD * D], BF16))
                stage = e6.enter_context(nc.sbuf_tensor("stage1", [128, 2 * LQ], BF16))

                def _(g, t_l0_done=t_l0_done, v_l0_done=v_l0_done):
                    g.wait_ge(tms.h, t_l0_done)
                    g.wait_ge(vms.h, v_l0_done)
                    for dst, srcw in [(wA, wq1), (wB, wk1), (wC, wv1), (wO, wo1)]:
                        g.dma_start(out=dst[:].rearrange("p (c o) -> p c o", c=NCD),
                                    in_=srcw.rearrange("(c p) o -> p c o", p=128)).then_inc(gW1.h, 16)
                p.on("gpsimd", _)
                g_w1v = gW1.inc(64)

                v_ln3 = emit_ln("ln3", e6)
                pps2 = [e6.enter_context(nc.psum_tensor(f"pq{i}", [128, 512])) for i in range(2)]
                kst1, vst1 = [], []
                emit_proj(wA, qTd, None, pps2, v_ln3, gW1, g_w1v, stage)
                emit_proj(wB, kin, None, pps2, v_ln3, gW1, g_w1v, stage, chunk_stores=kst1)
                s_vin1 = emit_projT(wC, None, pps2, v_ln3, gW1, g_w1v, stage,
                                    chunk_stores=vst1)

        if upto >= 6:
            def _(g, kst1=kst1, vst1=vst1):
                for c in range(8):
                    g.wait_ge(kst1[c][0].h, kst1[c][1])
                    g.collective_compute("AllGather", ALU.bypass,
                                         replica_groups=[[0, 1, 2, 3], [4, 5, 6, 7]],
                                         ins=[kin[128 * c:128 * (c + 1), :]],
                                         outs=[kg8[c]]).then_inc(ccK[1][c].h)
                for c in range(8):
                    for sv in vst1[c]:
                        g.wait_ge(sv[0].h, sv[1])
                    g.collective_compute("AllGather", ALU.bypass,
                                         replica_groups=[[0, 1, 2, 3], [4, 5, 6, 7]],
                                         ins=[vinT[c]],
                                         outs=[vgT8[c]]).then_inc(ccV[1][c].h)
            p.on("gpsimd", _)

            with contextlib.ExitStack() as e7:
                s_ao1 = emit_attention(1, e7, s_vin1, 1, 0)
        if upto >= 7:
            with contextlib.ExitStack() as e8:
                v_x3a = emit_oproj(e8, wO, s_ao1, gW1, g_w1v, 1)

        if upto >= 5:
            eW1.close()
        if upto >= 8:
         with contextlib.ExitStack() as e9:
            v_ln4 = emit_ln("ln4", e9)
            weff = e9.enter_context(nc.sbuf_tensor("weff", [128, NCD * D], BF16))
            lsc = e9.enter_context(nc.sbuf_tensor("lsc", [128, 2048], BF16))
            ltmp = e9.enter_context(nc.sbuf_tensor("ltmp", [128, 1024], F32))
            lw = e9.enter_context(nc.sbuf_tensor("lw", [128, 1128], BF16))
            ldax = e9.enter_context(nc.sbuf_tensor("ldax", [128, 1024], BF16))
            bbc = e9.enter_context(nc.sbuf_tensor("bbc_l1", [128, 1024], F32))
            brow_o = e9.enter_context(nc.sbuf_tensor("brow_l1", [1, D], F32))
            ppf = [e9.enter_context(nc.psum_tensor(f"pf{i}", [128, 512])) for i in range(4)]
            tg9, vg9 = tms.n, vms.n

            def _(a, tg9=tg9, vg9=vg9):
                a.wait_ge(tms.h, tg9)
                a.wait_ge(vms.h, vg9)
                a.dma_start(out=brow_o[:], in_=b1_in).then_inc(aB1.h, 16)
            p.on("scalar", _)
            a_b1 = aB1.inc(16)

            def _(g, tg9=tg9, vg9=vg9):
                g.wait_ge(tms.h, tg9)
                g.wait_ge(vms.h, vg9)
                g.dma_start(out=weff[:].rearrange("p (c o) -> p c o", c=NCD),
                            in_=wt1.rearrange("(c p) o -> p c o", p=128)).then_inc(gLW.h, 16)
                g.dma_start(out=lw[0:50, 0:1024], in_=uat).then_inc(gLW.h, 16)
                g.dma_start(out=lw[0:50, 1024:1028], in_=lup).then_inc(gLW.h, 16)
                g.dma_start(out=lw[0:4, 1028:1128], in_=ldn).then_inc(gLW.h, 16)
                g.dma_start(out=ldax[0:100, :], in_=dax).then_inc(gLW.h, 16)
            p.on("gpsimd", _)
            g_lw = gLW.inc(80)

            def _(te, g_lw=g_lw):
                te.wait_ge(gLW.h, g_lw)
                te.matmul(ppf[0][0:4, :], lw[0:50, 1024:1028], lw[0:50, 0:512], start=True, stop=True)
                te.matmul(ppf[1][0:4, :], lw[0:50, 1024:1028], lw[0:50, 512:1024],
                          start=True, stop=True).then_inc(tms.h, 1)
            p.on("tensor", _)
            t_t1 = tms.inc()

            def _(v, t_t1=t_t1):
                v.wait_ge(tms.h, t_t1)
                v.tensor_copy(out=lsc[0:4, 0:512], in_=ppf[0][0:4, :])
                v.tensor_copy(out=lsc[0:4, 512:1024], in_=ppf[1][0:4, :]).then_inc(vms.h, 1)
            p.on("vector", _)
            v_t1 = vms.inc()

            def _(te, v_t1=v_t1):
                te.wait_ge(vms.h, v_t1)
                te.matmul(ppf[2][0:100, :], lw[0:4, 1028:1128], lsc[0:4, 0:512], start=True, stop=True)
                te.matmul(ppf[3][0:100, :], lw[0:4, 1028:1128], lsc[0:4, 512:1024],
                          start=True, stop=True).then_inc(tms.h, 1)
            p.on("tensor", _)
            t_t2 = tms.inc()

            def _(v, t_t2=t_t2):
                v.wait_ge(tms.h, t_t2)
                v.tensor_copy(out=lsc[0:100, 1024:1536], in_=ppf[2][0:100, :])
                v.tensor_copy(out=lsc[0:100, 1536:2048], in_=ppf[3][0:100, :]).then_inc(vms.h, 1)
            p.on("vector", _)
            v_t2 = vms.inc()

            v_weff = 0
            for c in range(NCD):
                def _(te, c=c, v_t2=v_t2, v_weff=v_weff):
                    if c == 0:
                        te.wait_ge(vms.h, v_t2)
                    if c >= 2:
                        te.wait_ge(vms.h, v_weff)
                    te.matmul(ppf[c % 2][:], ldax[0:100, 128 * c:128 * (c + 1)],
                              lsc[0:100, 1024:1536], start=True, stop=True)
                    te.matmul(ppf[2 + c % 2][:], ldax[0:100, 128 * c:128 * (c + 1)],
                              lsc[0:100, 1536:2048], start=True, stop=True).then_inc(tms.h, 1)
                p.on("tensor", _)
                t_t3 = tms.inc()

                def _(v, c=c, t_t3=t_t3):
                    v.wait_ge(tms.h, t_t3)
                    v.tensor_add(out=weff[:, D * c:D * c + 512],
                                 in0=weff[:, D * c:D * c + 512], in1=ppf[c % 2][:])
                    v.tensor_add(out=weff[:, D * c + 512:D * (c + 1)],
                                 in0=weff[:, D * c + 512:D * (c + 1)],
                                 in1=ppf[2 + c % 2][:]).then_inc(vms.h, 1)
                p.on("vector", _)
                v_weff = vms.inc()

            v_b1bc = 0
            for hb in range(2):
                def _(te, hb=hb, v_weff=v_weff, v_b1bc=v_b1bc, a_b1=a_b1):
                    te.wait_ge(vms.h, v_weff if hb == 0 else v_b1bc)
                    te.wait_ge(aB1.h, a_b1)
                    te.matmul(ppf[hb][:], ones_r, brow_o[0:1, 512 * hb:512 * (hb + 1)],
                              start=True, stop=True).then_inc(tms.h, 1)
                p.on("tensor", _)
                t_bb = tms.inc()

                def _(v, hb=hb, t_bb=t_bb):
                    v.wait_ge(tms.h, t_bb)
                    v.tensor_copy(out=bbc[:, 512 * hb:512 * (hb + 1)],
                                  in_=ppf[hb][:]).then_inc(vms.h, 1)
                p.on("vector", _)
                v_b1bc = vms.inc()

            fin_ep = []
            for tt in range(NT):
                banks = [ppf[0], ppf[1]] if tt % 2 == 0 else [ppf[2], ppf[3]]
                warv = fin_ep[tt - 2] if tt >= 2 else 0

                def _(te, tt=tt, banks=banks, v_ln4=v_ln4, v_b1bc=v_b1bc, warv=warv):
                    if tt == 0:
                        te.wait_ge(vms.h, max(v_ln4, v_b1bc))
                    if warv:
                        te.wait_ge(vms.h, warv)
                    for n in range(2):
                        for ic in range(NCD):
                            ins = te.matmul(
                                banks[n][:],
                                lnT[:, LQ * ic + 128 * tt:LQ * ic + 128 * (tt + 1)],
                                weff[:, D * ic + 512 * n:D * ic + 512 * (n + 1)],
                                start=(ic == 0), stop=(ic == 7))
                    ins.then_inc(tms.h, 1)
                p.on("tensor", _)
                t_f = tms.inc()

                fin_base = vms.n
                prev_fin = fin_ep[-1] if fin_ep else 0

                def _(v, tt=tt, banks=banks, t_f=t_f, v_b1bc=v_b1bc, fin_base=fin_base,
                      prev_fin=prev_fin):
                    v.wait_ge(tms.h, t_f)
                    if prev_fin:
                        v.wait_ge(vms.h, prev_fin)
                    v.tensor_add(out=ltmp[:, 0:512], in0=banks[0][:], in1=bbc[:, 0:512])
                    v.tensor_add(out=ltmp[:, 512:1024], in0=banks[1][:],
                                 in1=bbc[:, 512:1024]).then_inc(vms.h, 1)
                    v.wait_ge(vms.h, fin_base + 1)
                    v.tensor_add(out=xs[:, 1024 * tt:1024 * tt + 512],
                                 in0=xs[:, 1024 * tt:1024 * tt + 512], in1=ltmp[:, 0:512])
                    v.tensor_add(out=xs[:, 1024 * tt + 512:1024 * (tt + 1)],
                                 in0=xs[:, 1024 * tt + 512:1024 * (tt + 1)],
                                 in1=ltmp[:, 512:1024]).then_inc(vms.h, 1)
                p.on("vector", _)
                vms.inc(1)
                fin_ep.append(vms.inc())
            pass

        v_final = vms.n

        def _(sy, v_final=v_final):
            sy.wait_ge(vms.h, v_final)
            sy.dma_start(out=out_d.rearrange("(t p) i -> p t i", p=128),
                         in_=sb_x3).then_inc(sOut.h, 16)
        p.on("sync", _)
        st_out = sOut.inc(16)

        if "lnT" in dump_outs:
            def _(g, dst=dump_outs["lnT"]):
                g.wait_ge(sOut.h, st_out)
                g.dma_start(out=dst.rearrange("p (c q) -> p c q", c=NCD) if False else dst,
                            in_=lnT[:]).then_inc(gLW.h, 16)
            p.on("gpsimd", _)
            gLW.inc(16)
        n_dumps = len(dump_outs)
        if n_dumps:
            srcmap = {"qT": qTd, "aoT": aoTd, "vin": vinT}
            gd_base = gLW.n

            def _(g, gd_base=gd_base):
                g.wait_ge(sOut.h, st_out)
                k = 0
                for name, dst in dump_outs.items():
                    if name not in srcmap:
                        continue
                    g.dma_start(out=dst, in_=srcmap[name]).then_inc(gLW.h, 16)
                    k += 1
                g.wait_ge(gLW.h, gd_base + 16 * k)
            p.on("gpsimd", _)
            for name in dump_outs:
                if name in ("qT", "aoT", "vin"):
                    gLW.inc(16)

        def _(sy):
            sy.wait_ge(sOut.h, st_out)
        p.on("sync", _)

        block = es.enter_context(nc.Block())
        p.materialize(block)
    return nc


def _prep_inputs(inputs):
    f32 = np.float32
    g = lambda k: np.asarray(inputs[k], dtype=f32)
    shared = {}
    shared["wqT0"] = np.ascontiguousarray(g("L0_attn_Wq").T)
    shared["wkT0"] = np.ascontiguousarray(g("L0_attn_Wk").T)
    shared["wvT0"] = np.ascontiguousarray(g("L0_attn_Wv").T)
    shared["woT0"] = np.ascontiguousarray(g("L0_attn_Wo").T)
    shared["bq0"] = np.ascontiguousarray(g("L0_attn_bq").reshape(8, 128).T)
    shared["bk0"] = np.ascontiguousarray(g("L0_attn_bk").reshape(8, 128).T)
    shared["bv0"] = np.ascontiguousarray(g("L0_attn_bv").reshape(8, 128).T)
    shared["bvr0"] = g("L0_attn_bv").reshape(1, D)
    shared["bo0"] = g("L0_attn_bo").reshape(1, D)
    gperm = np.concatenate([g("L0_attn_gamma")[0::2], g("L0_attn_gamma")[1::2]])
    shared["gamma"] = np.ascontiguousarray(gperm.reshape(1, 16))

    def blk(w):
        wt = np.ascontiguousarray(w.T)
        return np.ascontiguousarray(
            wt.reshape(wt.shape[0], 8, wt.shape[1] // 8).transpose(1, 0, 2))

    shared["w1b"] = blk(g("L0_ffn_w1_w"))
    shared["w2b"] = blk(g("L0_ffn_w2_w"))
    shared["h1b"] = blk(g("L0_ffn_h1_w"))
    shared["h2b"] = blk(g("L0_ffn_h2_w"))
    shared["w3T"] = np.ascontiguousarray(g("L0_ffn_w3_w").T)
    shared["s1"] = np.ascontiguousarray(g("L0_ffn_w1_s").reshape(32, 128).T)
    shared["s2"] = np.ascontiguousarray(g("L0_ffn_w2_s").reshape(32, 128).T)
    shared["s3"] = np.ascontiguousarray(g("L0_ffn_h1_s").reshape(32, 128).T)
    shared["s4"] = np.ascontiguousarray(g("L0_ffn_h2_s").reshape(32, 128).T)
    shared["s5"] = g("L0_ffn_w3_s").reshape(1, D)
    shared["wq1"] = np.ascontiguousarray(g("L1_attn_Wq"))
    shared["wk1"] = np.ascontiguousarray(g("L1_attn_Wk"))
    shared["wv1"] = np.ascontiguousarray(g("L1_attn_Wv"))
    shared["wo1"] = np.ascontiguousarray(g("L1_attn_Wo"))
    shared["bo1"] = g("L1_attn_bo").reshape(1, D)
    shared["wt1"] = np.ascontiguousarray(g("L1_ffn_W").T)
    shared["uat"] = np.ascontiguousarray(g("L1_ffn_up_aux").T)
    shared["lup"] = np.ascontiguousarray(g("L1_ffn_lora_up"))
    shared["ldn"] = np.ascontiguousarray(g("L1_ffn_lora_down"))
    shared["dax"] = np.ascontiguousarray(g("L1_ffn_down_aux"))
    shared["b1"] = g("L1_ffn_b").reshape(1, D)

    x = g("L0_attn_x")
    tri = np.triu(np.ones((128, 128), np.float32))
    in_maps = []
    row_idx = []
    for p in range(8):
        s, r = p // 4, p % 4
        rows = np.concatenate([np.arange(128 * (4 * lt + r), 128 * (4 * lt + r) + 128)
                               for lt in range(NT)])
        row_idx.append((s, rows))
        xsh = np.ascontiguousarray(x[s][rows])
        qpos = rows.astype(np.float32)
        qpd = np.ascontiguousarray(np.broadcast_to(qpos[None, :], (128, LQ)))
        masks = np.zeros((16, 128, 512), np.float32)
        for m in range(16):
            for mp in range(4):
                if m < 4 * mp + r:
                    masks[m, :, 128 * mp:128 * (mp + 1)] = 1.0
                elif m == 4 * mp + r:
                    masks[m, :, 128 * mp:128 * (mp + 1)] = tri
        im = {"x": xsh, "qpd": qpd, "masks": masks}
        im.update(shared)
        in_maps.append(im)
    return in_maps, row_idx


def kernel(**inputs):
    in_maps, row_idx = _prep_inputs(inputs)
    nc = build_program()
    res = run_bass_kernel_spmd(nc, in_maps, core_ids=list(range(8)))
    out = np.zeros((2, SEQ, D), np.float32)
    for p in range(8):
        s, rows = row_idx[p]
        out[s][rows] = res.results[p]["out"]
    return out


# version 10 (final)
# speedup vs baseline: 1.0354x; 1.0354x over previous
import contextlib
import numpy as np
import ml_dtypes
import concourse.bass as bass
import concourse.mybir as mybir
from concourse.bass_utils import run_bass_kernel_spmd

F32 = mybir.dt.float32
BF16 = mybir.dt.bfloat16
AF = mybir.ActivationFunctionType
ALU = mybir.AluOpType
AX = mybir.AxisListType
BFD = ml_dtypes.bfloat16

ENGINES = ("tensor", "vector", "scalar", "gpsimd", "sync")

D = 1024
LQ = 2048
NT = 16
NCD = 8
SEQ = 8192
NKT = 64
NPAIR = 8
DFF = 4096
EPS = 1e-5


class Sem:
    def __init__(self, handle):
        self.h = handle
        self.n = 0

    def inc(self, k=1):
        self.n += k
        return self.n


class Prog:
    def __init__(self):
        self.items = {e: [] for e in ENGINES}

    def on(self, eng, fn):
        self.items[eng].append(fn)

    def materialize(self, block):
        for e in ENGINES:
            fns = self.items[e]

            def run(engine, fns=fns):
                for fn in fns:
                    fn(engine)

            getattr(block, e)(run)


def build_program(dump=(), upto=8):
    nc = bass.Bass("TRN2", target_bir_lowering=False)
    p = Prog()

    def ein(name, shape, dtype=F32):
        return nc.dram_tensor(name, shape, dtype, kind="ExternalInput").ap()

    x_in = ein("x", [LQ, D])
    masks_in = ein("masks", [16, 128, 512])
    qpd_in = ein("qpd", [128, LQ])
    wqT0 = ein("wqT0", [D, D]); wkT0 = ein("wkT0", [D, D]); wvT0 = ein("wvT0", [D, D])
    woT0 = ein("woT0", [D, D])
    bq0 = ein("bq0", [128, NCD]); bk0 = ein("bk0", [128, NCD]); bv0 = ein("bv0", [128, NCD])
    bvr0 = ein("bvr0", [1, D])
    bo0 = ein("bo0", [1, D])
    gamma_in = ein("gamma", [1, 16])
    w1b = ein("w1b", [8, D, 512]); w2b = ein("w2b", [8, D, 512])
    h1b = ein("h1b", [8, DFF, 512]); h2b = ein("h2b", [8, DFF, 512])
    w3T_in = ein("w3T", [DFF, D])
    s1_in = ein("s1", [128, 32]); s2_in = ein("s2", [128, 32])
    s3_in = ein("s3", [128, 32]); s4_in = ein("s4", [128, 32])
    s5_in = ein("s5", [1, D])
    wq1 = ein("wq1", [D, D]); wk1 = ein("wk1", [D, D]); wv1 = ein("wv1", [D, D])
    wo1 = ein("wo1", [D, D]); bo1 = ein("bo1", [1, D])
    wt1 = ein("wt1", [D, D]); uat = ein("uat", [50, D]); lup = ein("lup", [50, 4])
    ldn = ein("ldn", [4, 100]); dax = ein("dax", [100, D]); b1_in = ein("b1", [1, D])

    out_d = nc.dram_tensor("out", [LQ, D], F32, kind="ExternalOutput").ap()

    def din(name, shape, dtype=BF16):
        return nc.dram_tensor(name, shape, dtype, kind="Internal").ap()

    _pad = nc.dram_tensor("dram_pad", [64 * 1024 * 1024], mybir.dt.uint8, kind="Internal")
    qTd = din("qTd", [D, LQ])
    kin = din("kin", [D, LQ]); vinT = din("vinT", [8, LQ, 128])
    kg8 = din("kg8", [8, 4 * 128, LQ]); vgT8 = din("vgT8", [8, 4, LQ, 128])
    aoTd = din("aoTd", [D, LQ])
    w1c = din("w1c", [8, D, 512]); w2c = din("w2c", [8, D, 512])
    h1c = din("h1c", [8, DFF, 512]); h2c = din("h2c", [8, DFF, 512])
    w3c = din("w3c", [DFF, D])

    dump_outs = {}
    for name in dump:
        shp, dt = {
            "lnT": ([128, NCD * LQ], F32), "qT": ([D, LQ], F32),
            "kg": ([4 * D, LQ], F32), "vg": ([4 * D, LQ], F32),
            "aoT": ([D, LQ], F32), "vin": ([8, LQ, 128], F32),
            "x1": ([LQ, D], F32), "x2": ([LQ, D], F32),
            "x3": ([LQ, D], F32),
        }[name]
        dump_outs[name] = nc.dram_tensor("dbg_" + name, shp, dt, kind="ExternalOutput").ap()

    ident_c = nc.inline_tensor(np.eye(128, dtype=BFD), "ident_c").ap()
    ones_row_c = nc.inline_tensor(np.ones((1, 128), np.float32), "ones_row_c").ap()
    sel_lo = np.zeros((1, 128), np.float32); sel_lo[0, :64] = 1.0
    sel_lo_c = nc.inline_tensor(sel_lo, "sel_lo_c").ap()
    sel_hi = np.zeros((1, 128), np.float32); sel_hi[0, 64:] = 1.0
    sel_hi_c = nc.inline_tensor(sel_hi, "sel_hi_c").ap()

    es = contextlib.ExitStack()
    with es:
        tms = Sem(es.enter_context(nc.semaphore("tms")))
        vms = Sem(es.enter_context(nc.semaphore("vms")))
        ams = Sem(es.enter_context(nc.semaphore("ams")))
        stx = Sem(es.enter_context(nc.semaphore("stx")))
        ccK = [[Sem(es.enter_context(nc.semaphore(f"ccK{l}_{c}"))) for c in range(8)]
               for l in range(2)]
        ccV = [[Sem(es.enter_context(nc.semaphore(f"ccV{l}_{c}"))) for c in range(8)]
               for l in range(2)]
        def mksem(name):
            return Sem(es.enter_context(nc.semaphore(name)))
        sX = mksem("sX"); sCst = mksem("sCst"); sMask = mksem("sMask")
        gMask = mksem("gMask")
        sCache = mksem("sCache"); sOut = mksem("sOut")
        gW0 = mksem("gW0"); gW1 = mksem("gW1"); gLW = mksem("gLW")
        aSmall = mksem("aSmall"); aBo0 = mksem("aBo0"); aBo1 = mksem("aBo1")
        aScl = mksem("aScl"); aB1 = mksem("aB1")
        projS = [mksem(f"projS{i}") for i in range(2)]
        kbS = [mksem(f"kbS{i}") for i in range(2)]
        vbS = [mksem(f"vbS{i}") for i in range(2)]
        qS = [mksem(f"qS{i}") for i in range(2)]
        stgS = [mksem(f"stgS{i}") for i in range(2)]
        aotS = mksem("aotS")
        wbsS = [mksem(f"wbsS{i}") for i in range(2)]
        w3S = [mksem(f"w3S{i}") for i in range(2)]
        vtS = [mksem(f"vtS{i}") for i in range(4)]

        xs = es.enter_context(nc.sbuf_tensor("xs", [128, NT * 1024], F32))
        cst = es.enter_context(nc.sbuf_tensor("cst", [128, 4], F32))
        ident = es.enter_context(nc.sbuf_tensor("ident", [128, 128], BF16))
        selr = es.enter_context(nc.sbuf_tensor("selr", [1, 384], F32))
        biasq = es.enter_context(nc.sbuf_tensor("biasq", [128, 3 * NCD], F32))
        bvrow_sb = es.enter_context(nc.sbuf_tensor("bvrow", [1, D], F32))
        bvbc = es.enter_context(nc.sbuf_tensor("bvbc_v", [128, D], F32))
        gam = es.enter_context(nc.sbuf_tensor("gam", [1, 32], F32))
        lgp = es.enter_context(nc.sbuf_tensor("lgp", [1, 16], F32))
        lngpair = es.enter_context(nc.sbuf_tensor("lngpair", [128, 8], F32))
        lnT = es.enter_context(nc.sbuf_tensor("lnT", [128, NCD * LQ], BF16))

        sb_x3 = xs[:].rearrange("p (t i) -> p t i", t=NT)
        ones_r = selr[0:1, 0:128]
        sel_l = selr[0:1, 128:256]
        sel_h = selr[0:1, 256:384]

        def _(sy):
            sy.dma_start(out=sb_x3, in_=x_in.rearrange("(t p) i -> p t i", p=128)).then_inc(sX.h, 16)
            sy.dma_start(out=ident[:], in_=ident_c).then_inc(sCst.h, 16)
            sy.dma_start(out=selr[:, 0:128], in_=ones_row_c).then_inc(sCst.h, 16)
            sy.dma_start(out=selr[:, 128:256], in_=sel_lo_c).then_inc(sCst.h, 16)
            sy.dma_start(out=selr[:, 256:384], in_=sel_hi_c).then_inc(sCst.h, 16)
        p.on("sync", _)
        s_x = sX.inc(16)
        s_const = sCst.inc(64)

        def _(a):
            for dst, src in [(biasq[:, 0:NCD], bq0), (biasq[:, NCD:2 * NCD], bk0),
                             (biasq[:, 2 * NCD:3 * NCD], bv0), (gam[:, 0:16], gamma_in)]:
                a.dma_start(out=dst, in_=src).then_inc(aSmall.h, 16)
            a.dma_start(out=bvrow_sb[:], in_=bvr0).then_inc(aSmall.h, 16)
        p.on("scalar", _)
        a_small = aSmall.inc(16 * 5)

        def _(v):
            v.memset(cst[:, 0:1], EPS)
            v.memset(cst[:, 1:2], 0.125).then_inc(vms.h, 1)
        p.on("vector", _)
        v_cst = vms.inc()

        def emit_ln(tag, esc):
            sums = esc.enter_context(nc.sbuf_tensor(f"lns_{tag}", [128, 4 * NT], F32))
            sqs = esc.enter_context(nc.sbuf_tensor(f"lnq_{tag}", [128, 1024], F32))
            sumsq = esc.enter_context(nc.sbuf_tensor(f"lnsq_{tag}", [128, NT], F32))
            lntm = esc.enter_context(nc.sbuf_tensor(f"lntm_{tag}", [128, 2 * 1024], BF16))
            tps = [esc.enter_context(nc.psum_tensor(f"lntp{i}_{tag}", [128, 512], BF16))
                   for i in range(2)]
            m = sums[:, NT:2 * NT]; var = sums[:, 2 * NT:3 * NT]; rstd = sums[:, 3 * NT:4 * NT]
            x_entry_v = vms.n
            is1 = tag == "ln1"

            def _(v, x_entry_v=x_entry_v):
                if is1:
                    v.wait_ge(sX.h, s_x)
                else:
                    v.wait_ge(vms.h, x_entry_v)
                v.tensor_reduce(out=sums[:, 0:NT], in_=sb_x3, axis=AX.X, op=ALU.add).then_inc(vms.h, 1)
            p.on("vector", _)
            v_sums = vms.inc()

            base_a = ams.n

            def _(a, x_entry_v=x_entry_v, base_a=base_a):
                if is1:
                    a.wait_ge(sX.h, s_x)
                else:
                    a.wait_ge(vms.h, x_entry_v)
                for t in range(NT):
                    if t:
                        a.wait_ge(ams.h, base_a + t)
                    a.activation(out=sqs[:], in_=xs[:, 1024 * t:1024 * (t + 1)],
                                 func=AF.Square,
                                 accum_out=sumsq[:, t:t + 1]).then_inc(ams.h, 1)
            p.on("scalar", _)
            ams.inc(NT - 1)
            a_sq = ams.inc()

            base_v = vms.n

            def _(v, a_sq=a_sq, base_v=base_v, v_sums=v_sums):
                v.wait_ge(ams.h, a_sq)
                v.wait_ge(vms.h, v_sums)
                v.tensor_scalar_mul(out=m, in0=sums[:, 0:NT], scalar1=1.0 / 1024.0).then_inc(vms.h, 1)
                v.tensor_scalar_mul(out=var, in0=sumsq[:], scalar1=1.0 / 1024.0).then_inc(vms.h, 1)
                v.wait_ge(vms.h, base_v + 2)
                v.tensor_mul(out=sumsq[:], in0=m, in1=m).then_inc(vms.h, 1)
                v.wait_ge(vms.h, base_v + 3)
                v.tensor_sub(out=var, in0=var, in1=sumsq[:]).then_inc(vms.h, 1)
            p.on("vector", _)
            vms.inc(3)
            v_var = vms.inc()

            def _(a, v_var=v_var):
                a.wait_ge(vms.h, max(v_var, v_cst))
                a.activation(out=rstd, in_=var, func=AF.Sqrt, bias=cst[:, 0:1]).then_inc(ams.h, 1)
            p.on("scalar", _)
            a_rt = ams.inc()

            base_v2 = vms.n

            def _(v, a_rt=a_rt, base_v2=base_v2):
                v.wait_ge(ams.h, a_rt)
                v.reciprocal(out=rstd, in_=rstd).then_inc(vms.h, 1)
                v.wait_ge(vms.h, base_v2 + 1)
                v.tensor_mul(out=m, in0=m, in1=rstd).then_inc(vms.h, 1)
                v.wait_ge(vms.h, base_v2 + 2)
                v.tensor_scalar_mul(out=m, in0=m, scalar1=-1.0).then_inc(vms.h, 1)
            p.on("vector", _)
            vms.inc(2)
            v_stat = vms.inc()

            v_aff = []
            tr_last = []
            evs = []
            for t in range(NT):
                warT = tr_last[t - 2] if t >= 2 else 0

                def _(v, t=t, warT=warT):
                    if t == 0:
                        v.wait_ge(vms.h, v_stat)
                    if warT:
                        v.wait_ge(tms.h, warT)
                    v.tensor_scalar(
                        out=lntm[:, 1024 * (t % 2):1024 * (t % 2 + 1)],
                        in0=xs[:, 1024 * t:1024 * (t + 1)],
                        scalar1=rstd[:, t:t + 1], scalar2=m[:, t:t + 1],
                        op0=ALU.mult, op1=ALU.add).then_inc(vms.h, 1)
                p.on("vector", _)
                v_aff.append(vms.inc())

                for g in range(2):
                    gi = t * 2 + g
                    bank = tps[gi % 2]
                    war = evs[gi - 2] if gi >= 2 else 0

                    def _(te, t=t, g=g, bank=bank, war=war):
                        te.wait_ge(vms.h, v_aff[t])
                        if t == 0 and g == 0:
                            te.wait_ge(sCst.h, s_const)
                        if war:
                            te.wait_ge(vms.h, war)
                        for ci in range(4):
                            c = g * 4 + ci
                            ins = te.transpose(
                                bank[:, 128 * ci:128 * (ci + 1)],
                                lntm[:, 1024 * (t % 2) + 128 * c:1024 * (t % 2) + 128 * (c + 1)],
                                ident[:])
                        ins.then_inc(tms.h, 1)
                    p.on("tensor", _)
                    t_tr = tms.inc()

                    def _(v, t=t, g=g, bank=bank, t_tr=t_tr):
                        v.wait_ge(tms.h, t_tr)
                        for ci in range(4):
                            c = g * 4 + ci
                            ins = v.tensor_copy(
                                out=lnT[:, LQ * c + 128 * t:LQ * c + 128 * (t + 1)],
                                in_=bank[:, 128 * ci:128 * (ci + 1)])
                        ins.then_inc(vms.h, 1)
                    p.on("vector", _)
                    evs.append(vms.inc())
                tr_last.append(t_tr)
            return vms.n

        proj_ev = []

        def emit_proj(wT_sb, out_dram, bias_ap, pss, v_lnT, w_sem, w_val, stage,
                      chunk_stores=None):
            if chunk_stores is None:
                chunk_stores = []
            ev_done = proj_ev
            first = [True]
            for c in range(NCD):
                slot = c % 2
                stage_war = projS[slot].n
                for s in range(4):
                    idx = len(ev_done)
                    bank = pss[(c * 4 + s) % 2]
                    warv = ev_done[idx - 2] if idx >= 2 else 0
                    fst = first[0]; first[0] = False

                    def _(te, c=c, s=s, bank=bank, warv=warv, fst=fst):
                        if fst:
                            te.wait_ge(vms.h, v_lnT)
                            te.wait_ge(w_sem.h, w_val)
                        if warv:
                            te.wait_ge(vms.h, warv)
                        for ic in range(NCD):
                            ins = te.matmul(
                                bank[:],
                                wT_sb[:, D * ic + 128 * c:D * ic + 128 * (c + 1)],
                                lnT[:, LQ * ic + 512 * s:LQ * ic + 512 * (s + 1)],
                                start=(ic == 0), stop=(ic == NCD - 1))
                        ins.then_inc(tms.h, 1)
                    p.on("tensor", _)
                    t_mm = tms.inc()

                    def _(v, c=c, s=s, bank=bank, t_mm=t_mm, stage_war=stage_war, slot=slot):
                        v.wait_ge(tms.h, t_mm)
                        if s == 0:
                            if stage_war:
                                v.wait_ge(projS[slot].h, stage_war)
                            if bias_ap is not None:
                                v.wait_ge(aSmall.h, a_small)
                        if bias_ap is not None:
                            v.tensor_scalar_add(
                                out=stage[:, LQ * slot + 512 * s:LQ * slot + 512 * (s + 1)],
                                in0=bank[:], scalar1=bias_ap[:, c:c + 1]).then_inc(vms.h, 1)
                        else:
                            v.tensor_copy(
                                out=stage[:, LQ * slot + 512 * s:LQ * slot + 512 * (s + 1)],
                                in_=bank[:]).then_inc(vms.h, 1)
                    p.on("vector", _)
                    ev_done.append(vms.inc())

                def _(sy, c=c, slot=slot, ev=ev_done[-1]):
                    sy.wait_ge(vms.h, ev)
                    sy.dma_start(out=out_dram[128 * c:128 * (c + 1), :],
                                 in_=stage[:, LQ * slot:LQ * (slot + 1)]).then_inc(projS[slot].h, 16)
                p.on("sync", _)
                chunk_stores.append((projS[slot], projS[slot].inc(16)))
            return (projS[0].n, projS[1].n)

        def emit_projT(wsb, bvbc, pss, v_lnT, w_sem, w_val, stage, chunk_stores):
            ev_done = proj_ev
            first = [True]
            for n in range(2):
                for tg in range(4):
                    slot = (n * 4 + tg) % 2
                    stage_war = projS[slot].n
                    for ti in range(4):
                        t = 4 * tg + ti
                        idx = len(ev_done)
                        bank = pss[idx % 2]
                        warv = ev_done[idx - 2] if idx >= 2 else 0
                        fst = first[0]; first[0] = False

                        def _(te, n=n, t=t, bank=bank, warv=warv, fst=fst):
                            if fst:
                                te.wait_ge(vms.h, v_lnT)
                                te.wait_ge(w_sem.h, w_val)
                            if warv:
                                te.wait_ge(vms.h, warv)
                            for ic in range(NCD):
                                ins = te.matmul(
                                    bank[:],
                                    lnT[:, LQ * ic + 128 * t:LQ * ic + 128 * (t + 1)],
                                    wsb[:, D * ic + 512 * n:D * ic + 512 * (n + 1)],
                                    start=(ic == 0), stop=(ic == NCD - 1))
                            ins.then_inc(tms.h, 1)
                        p.on("tensor", _)
                        t_mm = tms.inc()

                        def _(v, n=n, ti=ti, bank=bank, t_mm=t_mm,
                              stage_war=stage_war, slot=slot):
                            v.wait_ge(tms.h, t_mm)
                            if ti == 0 and stage_war:
                                v.wait_ge(projS[slot].h, stage_war)
                            if bvbc is not None:
                                v.tensor_tensor(
                                    out=stage[:, LQ * slot + 512 * ti:LQ * slot + 512 * (ti + 1)],
                                    in0=bank[:], in1=bvbc[:, 512 * n:512 * (n + 1)],
                                    op=ALU.add).then_inc(vms.h, 1)
                            else:
                                v.tensor_copy(
                                    out=stage[:, LQ * slot + 512 * ti:LQ * slot + 512 * (ti + 1)],
                                    in_=bank[:]).then_inc(vms.h, 1)
                        p.on("vector", _)
                        ev_done.append(vms.inc())

                    def _(sy, n=n, tg=tg, slot=slot, ev=ev_done[-1]):
                        sy.wait_ge(vms.h, ev)
                        sview = stage[:, LQ * slot:LQ * slot + 2048].rearrange(
                            "p (ti q e) -> p ti q e", ti=4, q=4)
                        for q in range(4):
                            sy.dma_start(
                                out=vinT[4 * n + q, 512 * tg:512 * (tg + 1), :].rearrange(
                                    "(ti pp) e -> pp ti e", pp=128),
                                in_=sview[:, :, q, :]).then_inc(projS[slot].h, 16)
                    p.on("sync", _)
                    projS[slot].inc(64)
                both = [(projS[0], projS[0].n), (projS[1], projS[1].n)]
                for q in range(4):
                    chunk_stores.append(both)
            return (projS[0].n, projS[1].n)

        def emit_attention(layer, esc, s_q_done, lix, v_pre):
            t_entry = tms.n
            v_entry = vms.n
            kb = [esc.enter_context(nc.sbuf_tensor(f"kb{i}_{layer}", [128, 2048], BF16))
                  for i in range(2)]
            qsbs = [esc.enter_context(nc.sbuf_tensor(f"qsb{i}_{layer}", [128, LQ], BF16))
                    for i in range(2)]
            pt = esc.enter_context(nc.sbuf_tensor(f"pt_{layer}", [128, 4 * 1024], BF16))
            stg = esc.enter_context(nc.sbuf_tensor(f"stg_{layer}", [128, 2 * 512], BF16))
            if layer == 0:
                vt = esc.enter_context(nc.sbuf_tensor("vt0", [128, NKT * 128], BF16))
                masks_sb = esc.enter_context(nc.sbuf_tensor("masks_sb", [128, 16 * 512], BF16))
                qpd_sb = esc.enter_context(nc.sbuf_tensor("qpd_sb", [128, LQ], F32))
                dps = esc.enter_context(nc.sbuf_tensor("dps", [128, 2 * 512], F32))
                stg2 = None
            else:
                vt = esc.enter_context(nc.sbuf_tensor("vaug1", [128, NKT * 130], BF16))
                lrow = esc.enter_context(nc.sbuf_tensor("lrow", [1, 1024], F32))
                rlbc = esc.enter_context(nc.sbuf_tensor("rlbc", [128, 2 * 512], F32))
                stg2 = esc.enter_context(nc.sbuf_tensor("stg2_1", [128, 2 * 512], BF16))
            pAB = [esc.enter_context(nc.psum_tensor(f"pAB{i}_{layer}", [128, 1024]))
                   for i in range(3)]
            pC1 = esc.enter_context(nc.psum_tensor(f"pC1_{layer}", [128, 512]))
            if layer == 1:
                pC2 = esc.enter_context(nc.psum_tensor(f"pC2_{layer}", [128, 512]))

            v_ones = 0
            s_mask = 0
            v_lngpair = 0
            if layer == 0:
                def _(g, t_entry=t_entry, v_entry=v_entry, p0=projS[0].n, p1=projS[1].n):
                    g.wait_ge(tms.h, t_entry)
                    g.wait_ge(vms.h, v_entry)
                    g.wait_ge(projS[0].h, p0)
                    g.wait_ge(projS[1].h, p1)
                    g.dma_start(out=masks_sb[:].rearrange("p (m q) -> p m q", m=16),
                                 in_=masks_in.rearrange("m p q -> p m q")).then_inc(gMask.h, 16)
                p.on("gpsimd", _)
                g_mask = gMask.inc(16)

                def _(sy, t_entry=t_entry, v_entry=v_entry, p0=projS[0].n, p1=projS[1].n):
                    sy.wait_ge(tms.h, t_entry)
                    sy.wait_ge(vms.h, v_entry)
                    sy.wait_ge(projS[0].h, p0)
                    sy.wait_ge(projS[1].h, p1)
                    sy.dma_start(out=qpd_sb[:], in_=qpd_in).then_inc(sMask.h, 16)
                p.on("sync", _)
                s_mask = sMask.inc(16)

                def _(a):
                    a.wait_ge(aSmall.h, a_small)
                    a.activation(out=gam[:, 16:32], in_=gam[:, 0:16], func=AF.Ln).then_inc(ams.h, 1)
                p.on("scalar", _)
                a_lng = ams.inc()

                def _(te, a_lng=a_lng, t_entry=t_entry):
                    te.wait_ge(ams.h, a_lng)
                    te.wait_ge(sCst.h, s_const)
                    te.matmul(pAB[0][:, 0:8], sel_l, gam[0:1, 16:24], start=True, stop=False)
                    te.matmul(pAB[0][:, 0:8], sel_h, gam[0:1, 24:32], start=False, stop=True).then_inc(tms.h, 1)
                p.on("tensor", _)
                t_lgp = tms.inc()

                def _(v, t_lgp=t_lgp):
                    v.wait_ge(tms.h, t_lgp)
                    v.tensor_copy(out=lngpair[:], in_=pAB[0][:, 0:8]).then_inc(vms.h, 1)
                p.on("vector", _)
                v_lngpair = vms.inc()
            else:
                def _(v, t_entry=t_entry):
                    v.wait_ge(tms.h, t_entry)
                    va = vt[:].rearrange("p (t g c) -> p t g c", t=NKT, g=2)
                    v.memset(va[:, :, :, 64:65], 1.0).then_inc(vms.h, 1)
                p.on("vector", _)
                v_ones = vms.inc()

            q_loaded = [0] * NPAIR
            last_q_use = [0] * NPAIR
            nkload = [0]

            def load_q(p2):
                qwar = last_q_use[p2 - 1] if p2 >= 1 else 0

                def _(sy, p2=p2, qwar=qwar, t_entry=t_entry, v_entry=v_entry):
                    sy.wait_ge(ccK[lix][p2].h, 1)
                    if p2 == 0:
                        sy.wait_ge(projS[0].h, s_q_done[0])
                        sy.wait_ge(projS[1].h, s_q_done[1])
                        sy.wait_ge(tms.h, t_entry)
                        sy.wait_ge(vms.h, v_entry)
                    if qwar:
                        sy.wait_ge(tms.h, qwar)
                    sy.dma_start(out=qsbs[p2 % 2][:], in_=qTd[128 * p2:128 * (p2 + 1), :]).then_inc(qS[p2 % 2].h, 16)
                p.on("sync", _)
                q_loaded[p2] = (qS[p2 % 2], qS[p2 % 2].inc(16))

            def load_kband(p2, w, war):
                par = nkload[0] % 2
                nkload[0] += 1

                def _(sy, p2=p2, w=w, war=war, par=par):
                    sy.wait_ge(ccK[lix][p2].h, 1)
                    if war:
                        sy.wait_ge(tms.h, war)
                    for r in range(4):
                        sy.dma_start(out=kb[par][:, 512 * r:512 * (r + 1)],
                                     in_=kg8[p2][128 * r:128 * (r + 1),
                                                 512 * w:512 * (w + 1)]).then_inc(kbS[par].h, 16)
                p.on("sync", _)
                return par, (kbS[par], kbS[par].inc(64))

            load_q(0)
            stg_war = [0, 0]
            epil_done = 0
            kb_use = []
            ptev = []
            pv_done = []

            def emit_vwindow(p2, w, pv_guard):
                def _(sy, p2=p2, w=w, pv_guard=pv_guard,
                      vo=(v_ones if layer == 1 else 0)):
                    sy.wait_ge(ccV[lix][p2].h, 1)
                    if pv_guard:
                        sy.wait_ge(tms.h, pv_guard)
                    if vo:
                        sy.wait_ge(vms.h, vo)
                    for r in range(4):
                        srcw = vgT8[p2, r, 512 * w:512 * (w + 1), :]
                        if layer == 0:
                            dstw = vt[:].rearrange("p (lt r e) -> p lt r e",
                                                   r=4, e=128)[:, 4 * w:4 * (w + 1), r, :]
                            sy.dma_start(out=dstw,
                                         in_=srcw.rearrange("(i p) e -> p i e",
                                                            p=128)).then_inc(vtS[w].h, 16)
                        else:
                            for gg in range(2):
                                dstw = vt[:].rearrange(
                                    "p (lt r g e) -> p lt r g e",
                                    r=4, g=2, e=65)[:, 4 * w:4 * (w + 1), r, gg, 0:64]
                                sy.dma_start(
                                    out=dstw,
                                    in_=srcw.rearrange("(i p) (g e) -> p i g e",
                                                       p=128, g=2)[:, :, gg, :]).then_inc(vtS[w].h, 16)
                p.on("sync", _)
                return (vtS[w], vtS[w].inc(64 if layer == 0 else 128))

            for p2 in range(NPAIR):
                if p2 + 1 < NPAIR:
                    load_q(p2 + 1)

                pv_guard = pv_done[-1] if pv_done else 0
                win_done = {}
                for qc in range(4):
                    windows = list(range(qc + 1)) if layer == 0 else [0, 1, 2, 3]

                    first_j = True
                    qc_base = len(ptev)
                    pend = []

                    def emit_pv(rec, epw):
                        idx, j, ppar, trj, first, last = rec
                        ev_list = ptev[idx]

                        def _(te, j=j, ppar=ppar, ev_list=ev_list, trj=trj,
                              first=first, last=last, epw=epw, layer=layer):
                            for _s, _v in ev_list:
                                te.wait_ge(_s.h, _v)
                            te.wait_ge(trj[0].h, trj[1])
                            if epw:
                                te.wait_ge(vms.h, epw)
                            h1 = pt[:, 1024 * ppar:1024 * ppar + 512]
                            h2 = pt[:, 1024 * ppar + 512:1024 * (ppar + 1)]
                            if layer == 0:
                                te.matmul(pC1[0:64, :], vt[:, 128 * j:128 * j + 64],
                                          h1, start=first, stop=last,
                                          tile_position=(0, 0), skip_group_check=True)
                                ins = te.matmul(pC1[64:128, :],
                                                vt[:, 128 * j + 64:128 * (j + 1)],
                                                h2, start=first, stop=last,
                                                tile_position=(0, 64),
                                                skip_group_check=True)
                            else:
                                te.matmul(pC1[0:65, :], vt[:, 130 * j:130 * j + 65],
                                          h1, start=first, stop=last)
                                ins = te.matmul(pC2[0:65, :],
                                                vt[:, 130 * j + 65:130 * (j + 1)],
                                                h2, start=first, stop=last)
                            ins.then_inc(tms.h, 1)
                        p.on("tensor", _)
                        pv_done.append(tms.inc())

                    for w in windows:
                        if w not in win_done:
                            win_done[w] = emit_vwindow(p2, w, pv_guard)
                        kwar = kb_use[-2] if len(kb_use) >= 2 else 0
                        kpar, k_ld = load_kband(p2, w, kwar)
                        for jj in range(16):
                            j = 16 * w + jj
                            idx = len(ptev)
                            par = idx % 3
                            ppar = idx % 4
                            warp = ptev[idx - 3] if idx >= 3 else None
                            ep_war = epil_done if (first_j and layer == 1) else 0
                            vlp = (v_lngpair if (layer == 0 and p2 == 0 and qc == 0
                                                 and idx == 0) else 0)
                            kcol = 512 * (j % 4) + 128 * ((j // 4) % 4)

                            def _(te, p2=p2, qc=qc, idx=idx, par=par, warp=warp,
                                  ep_war=ep_war, vlp=vlp, kpar=kpar,
                                  k_ld=k_ld, kcol=kcol, fj=first_j):
                                if fj:
                                    ql = q_loaded[p2]
                                    te.wait_ge(ql[0].h, ql[1])
                                te.wait_ge(k_ld[0].h, k_ld[1])
                                if warp:
                                    for _s, _v in warp:
                                        te.wait_ge(_s.h, _v)
                                if ep_war:
                                    te.wait_ge(vms.h, ep_war)
                                if vlp:
                                    te.wait_ge(vms.h, vlp)
                                te.matmul(pAB[par][:, 0:512],
                                          kb[kpar][0:64, kcol:kcol + 128],
                                          qsbs[p2 % 2][0:64, 512 * qc:512 * (qc + 1)],
                                          start=True, stop=True)
                                te.matmul(pAB[par][:, 512:1024],
                                          kb[kpar][64:128, kcol:kcol + 128],
                                          qsbs[p2 % 2][64:128, 512 * qc:512 * (qc + 1)],
                                          start=True, stop=True,
                                          tile_position=(64, 0)).then_inc(tms.h, 1)
                            p.on("tensor", _)
                            first_j = False
                            t_qk = tms.inc()
                            last_q_use[p2] = t_qk

                            pvw = pv_done[idx - 4] if idx >= 4 else 0
                            if layer == 0 and w == qc:
                                mcol = 512 * jj

                                def _(v, par=par, ppar=ppar, t_qk=t_qk, mcol=mcol,
                                      pvw=pvw, idx=idx):
                                    v.wait_ge(tms.h, t_qk)
                                    if pvw:
                                        v.wait_ge(tms.h, pvw)
                                    if idx == 0:
                                        v.wait_ge(gMask.h, g_mask)
                                    v.tensor_tensor(out=pt[:, 1024 * ppar:1024 * ppar + 512],
                                                    in0=pAB[par][:, 0:512],
                                                    in1=masks_sb[:, mcol:mcol + 512],
                                                    op=ALU.mult)
                                    v.tensor_tensor(out=pt[:, 1024 * ppar + 512:1024 * (ppar + 1)],
                                                    in0=pAB[par][:, 512:1024],
                                                    in1=masks_sb[:, mcol:mcol + 512],
                                                    op=ALU.mult).then_inc(vms.h, 1)
                                p.on("vector", _)
                                ptev.append([(vms, vms.inc())])
                            elif layer == 1:
                                def _(a, par=par, ppar=ppar, t_qk=t_qk, pvw=pvw, idx=idx):
                                    a.wait_ge(tms.h, t_qk)
                                    if pvw:
                                        a.wait_ge(tms.h, pvw)
                                    if idx == 0:
                                        a.wait_ge(vms.h, v_cst)
                                    a.activation(out=pt[:, 1024 * ppar:1024 * (ppar + 1)],
                                                 in_=pAB[par][:], func=AF.Exp,
                                                 scale=cst[:, 1:2]).then_inc(ams.h, 1)
                                p.on("scalar", _)
                                ptev.append([(ams, ams.inc())])
                            else:
                                def _(v, par=par, ppar=ppar, t_qk=t_qk, pvw=pvw):
                                    v.wait_ge(tms.h, t_qk)
                                    if pvw:
                                        v.wait_ge(tms.h, pvw)
                                    v.tensor_copy(out=pt[:, 1024 * ppar:1024 * ppar + 512],
                                                  in_=pAB[par][:, 0:512]).then_inc(vms.h, 1)
                                p.on("vector", _)

                                def _(a, par=par, ppar=ppar, t_qk=t_qk, pvw=pvw):
                                    a.wait_ge(tms.h, t_qk)
                                    if pvw:
                                        a.wait_ge(tms.h, pvw)
                                    a.activation(out=pt[:, 1024 * ppar + 512:1024 * (ppar + 1)],
                                                 in_=pAB[par][:, 512:1024],
                                                 func=AF.Identity).then_inc(ams.h, 1)
                                p.on("scalar", _)
                                ptev.append([(vms, vms.inc()), (ams, ams.inc())])

                            pend.append((idx, j, ppar, win_done[w],
                                         idx == qc_base,
                                         w == windows[-1] and jj == 15))
                            if len(pend) > 3:
                                rec = pend.pop(0)
                                emit_pv(rec, epil_done if rec[4] else 0)
                        kb_use.append(tms.n)
                    while pend:
                        rec = pend.pop(0)
                        emit_pv(rec, epil_done if rec[4] else 0)

                    t_pv_last = pv_done[-1]
                    slot = (p2 * 4 + qc) % 2
                    if layer == 0:
                        dpw = epil_done

                        def _(a, p2=p2, qc=qc, dpw=dpw, v_lngpair=v_lngpair):
                            a.wait_ge(vms.h, v_lngpair)
                            a.wait_ge(sMask.h, s_mask)
                            if dpw:
                                a.wait_ge(vms.h, dpw)
                            a.activation(out=dps[:, 512 * (qc % 2):512 * (qc % 2 + 1)],
                                         in_=qpd_sb[:, 512 * qc:512 * (qc + 1)],
                                         func=AF.Exp, scale=lngpair[:, p2:p2 + 1]).then_inc(ams.h, 1)
                        p.on("scalar", _)
                        a_dp = ams.inc()

                        def _(v, qc=qc, slot=slot, a_dp=a_dp, t_pv_last=t_pv_last,
                              war=stg_war[slot]):
                            v.wait_ge(ams.h, a_dp)
                            v.wait_ge(tms.h, t_pv_last)
                            if war:
                                v.wait_ge(stgS[slot].h, war)
                            v.tensor_tensor(out=stg[:, 512 * slot:512 * (slot + 1)],
                                            in0=pC1[:],
                                            in1=dps[:, 512 * (qc % 2):512 * (qc % 2 + 1)],
                                            op=ALU.mult).then_inc(vms.h, 1)
                        p.on("vector", _)
                        epil_done = vms.inc()

                        def _(sy, p2=p2, qc=qc, slot=slot, v_stg=epil_done):
                            sy.wait_ge(vms.h, v_stg)
                            sy.dma_start(out=aoTd[128 * p2:128 * (p2 + 1), 512 * qc:512 * (qc + 1)],
                                         in_=stg[:, 512 * slot:512 * (slot + 1)]).then_inc(stgS[slot].h, 16)
                        p.on("sync", _)
                        stg_war[slot] = stgS[slot].inc(16)
                    else:
                        base_lr = vms.n

                        def _(v, t_pv_last=t_pv_last, base_lr=base_lr):
                            v.wait_ge(tms.h, t_pv_last)
                            v.tensor_copy(out=lrow[0:1, 0:512], in_=pC1[64:65, :]).then_inc(vms.h, 1)
                            v.tensor_copy(out=lrow[0:1, 512:1024], in_=pC2[64:65, :]).then_inc(vms.h, 1)
                            v.wait_ge(vms.h, base_lr + 2)
                            v.reciprocal(out=lrow[:], in_=lrow[:]).then_inc(vms.h, 1)
                        p.on("vector", _)
                        vms.inc(2)
                        v_lr = vms.inc()

                        def _(te, v_lr=v_lr):
                            te.wait_ge(vms.h, v_lr)
                            te.wait_ge(sCst.h, s_const)
                            te.matmul(pAB[0][:, 0:512], sel_l, lrow[0:1, 0:512], start=True, stop=True)
                            te.matmul(pAB[0][:, 512:1024], sel_l, lrow[0:1, 512:1024],
                                      start=True, stop=True).then_inc(tms.h, 1)
                        p.on("tensor", _)
                        t_rl = tms.inc()
                        last_q_use[p2] = max(last_q_use[p2], t_rl)

                        base_rl = vms.n

                        def _(v, slot=slot, t_rl=t_rl, war=stg_war[slot], base_rl=base_rl):
                            v.wait_ge(tms.h, t_rl)
                            v.tensor_copy(out=rlbc[:, 0:512], in_=pAB[0][:, 0:512]).then_inc(vms.h, 1)
                            v.tensor_copy(out=rlbc[:, 512:1024], in_=pAB[0][:, 512:1024]).then_inc(vms.h, 1)
                            if war:
                                v.wait_ge(stgS[slot].h, war)
                            v.wait_ge(vms.h, base_rl + 2)
                            v.tensor_tensor(out=stg[0:64, 512 * slot:512 * (slot + 1)],
                                            in0=pC1[0:64, :], in1=rlbc[0:64, 0:512],
                                            op=ALU.mult)
                            v.tensor_tensor(out=stg2[0:64, 512 * slot:512 * (slot + 1)],
                                            in0=pC2[0:64, :], in1=rlbc[0:64, 512:1024],
                                            op=ALU.mult).then_inc(vms.h, 1)
                        p.on("vector", _)
                        vms.inc(2)
                        epil_done = vms.inc()

                        def _(sy, p2=p2, qc=qc, slot=slot, v_stg=epil_done):
                            sy.wait_ge(vms.h, v_stg)
                            sy.dma_start(out=aoTd[128 * p2:128 * p2 + 64, 512 * qc:512 * (qc + 1)],
                                         in_=stg[0:64, 512 * slot:512 * (slot + 1)]).then_inc(stgS[slot].h, 16)
                            sy.dma_start(out=aoTd[128 * p2 + 64:128 * (p2 + 1), 512 * qc:512 * (qc + 1)],
                                         in_=stg2[0:64, 512 * slot:512 * (slot + 1)]).then_inc(stgS[slot].h, 16)
                        p.on("sync", _)
                        stg_war[slot] = stgS[slot].inc(32)
            return (stgS[0].n, stgS[1].n)

        def emit_oproj(esc, wo_sb, s_ao, w_sem, w_val, layer):
            aot = esc.enter_context(nc.sbuf_tensor(f"aot_{layer}", [128, 4 * LQ], BF16))
            tmp = esc.enter_context(nc.sbuf_tensor(f"otmp_{layer}", [128, 1024], F32))
            bbc = esc.enter_context(nc.sbuf_tensor(f"bbc_{layer}", [128, 1024], F32))
            brow_o = esc.enter_context(nc.sbuf_tensor(f"brow_{layer}", [1, D], F32))
            pOs = [[esc.enter_context(nc.psum_tensor(f"pO{2*i+n}_{layer}", [128, 512]))
                    for n in range(2)] for i in range(2)]
            pBB = esc.enter_context(nc.psum_tensor(f"pBB_{layer}", [128, 512]))
            tg, vg_ = tms.n, vms.n

            def _(a, tg=tg, vg_=vg_, s_ao=s_ao):
                a.wait_ge(tms.h, tg)
                a.wait_ge(vms.h, vg_)
                a.wait_ge(stgS[0].h, s_ao[0])
                a.wait_ge(stgS[1].h, s_ao[1])
                sbo = aBo0 if layer == 0 else aBo1
                a.dma_start(out=brow_o[:], in_=(bo0 if layer == 0 else bo1)).then_inc(sbo.h, 16)
            p.on("scalar", _)
            a_bo = ((aBo0 if layer == 0 else aBo1), (aBo0 if layer == 0 else aBo1).inc(16))

            v_bbc = 0
            for half_b in range(2):
                def _(te, half_b=half_b, v_bbc=v_bbc, a_bo=a_bo, s_ao=s_ao):
                    te.wait_ge(a_bo[0].h, a_bo[1])
                    te.wait_ge(stgS[0].h, s_ao[0])
                    te.wait_ge(stgS[1].h, s_ao[1])
                    te.wait_ge(sCst.h, s_const)
                    if v_bbc:
                        te.wait_ge(vms.h, v_bbc)
                    te.matmul(pBB[:], ones_r,
                              brow_o[0:1, 512 * half_b:512 * (half_b + 1)],
                              start=True, stop=True).then_inc(tms.h, 1)
                p.on("tensor", _)
                t_b = tms.inc()

                def _(v, half_b=half_b, t_b=t_b):
                    v.wait_ge(tms.h, t_b)
                    v.tensor_copy(out=bbc[:, 512 * half_b:512 * (half_b + 1)],
                                  in_=pBB[:]).then_inc(vms.h, 1)
                p.on("vector", _)
                v_bbc = vms.inc()

            ep_done = []
            t_half_last = 0
            for half in range(2):
                def _(sy, half=half, s_ao=s_ao, t_half_last=t_half_last):
                    sy.wait_ge(stgS[0].h, s_ao[0])
                    sy.wait_ge(stgS[1].h, s_ao[1])
                    if t_half_last:
                        sy.wait_ge(tms.h, t_half_last)
                    for cl in range(4):
                        c2 = 4 * half + cl
                        sy.dma_start(out=aot[:, LQ * cl:LQ * (cl + 1)],
                                     in_=aoTd[128 * c2:128 * (c2 + 1), :]).then_inc(aotS.h, 16)
                p.on("sync", _)
                s_load = aotS.inc(64)
                for tt in range(NT):
                    idx = half * NT + tt
                    banks = pOs[idx % 2]
                    warv = ep_done[idx - 2] if idx >= 2 else 0

                    def _(te, half=half, tt=tt, idx=idx, banks=banks, s_load=s_load,
                          warv=warv):
                        if idx == 0:
                            te.wait_ge(w_sem.h, w_val)
                        te.wait_ge(aotS.h, s_load)
                        if warv:
                            te.wait_ge(vms.h, warv)
                        for n in range(2):
                            for cl in range(4):
                                ins = te.matmul(
                                    banks[n][:],
                                    aot[:, LQ * cl + 128 * tt:LQ * cl + 128 * (tt + 1)],
                                    wo_sb[:, D * (4 * half + cl) + 512 * n:
                                          D * (4 * half + cl) + 512 * (n + 1)],
                                    start=(cl == 0), stop=(cl == 3))
                        ins.then_inc(tms.h, 1)
                    p.on("tensor", _)
                    t_o = tms.inc()
                    t_half_last = t_o

                    ep_base = vms.n
                    prev_ep = ep_done[-1] if ep_done else 0

                    def _(v, half=half, tt=tt, banks=banks, t_o=t_o, v_bbc=v_bbc,
                          ep_base=ep_base, prev_ep=prev_ep):
                        v.wait_ge(tms.h, t_o)
                        if half == 1 and prev_ep:
                            v.wait_ge(vms.h, prev_ep)
                        if half == 0:
                            v.tensor_add(out=xs[:, 1024 * tt:1024 * tt + 512],
                                         in0=xs[:, 1024 * tt:1024 * tt + 512], in1=banks[0][:])
                            v.tensor_add(out=xs[:, 1024 * tt + 512:1024 * (tt + 1)],
                                         in0=xs[:, 1024 * tt + 512:1024 * (tt + 1)],
                                         in1=banks[1][:]).then_inc(vms.h, 1)
                        else:
                            v.wait_ge(vms.h, v_bbc)
                            v.tensor_add(out=tmp[:, 0:512], in0=banks[0][:], in1=bbc[:, 0:512])
                            v.tensor_add(out=tmp[:, 512:1024], in0=banks[1][:],
                                         in1=bbc[:, 512:1024]).then_inc(vms.h, 1)
                            v.wait_ge(vms.h, ep_base + 1)
                            v.tensor_add(out=xs[:, 1024 * tt:1024 * tt + 512],
                                         in0=xs[:, 1024 * tt:1024 * tt + 512], in1=tmp[:, 0:512])
                            v.tensor_add(out=xs[:, 1024 * tt + 512:1024 * (tt + 1)],
                                         in0=xs[:, 1024 * tt + 512:1024 * (tt + 1)],
                                         in1=tmp[:, 512:1024]).then_inc(vms.h, 1)
                    p.on("vector", _)
                    if half == 1:
                        vms.inc(1)
                    ep_done.append(vms.inc())
            return vms.n

        def emit_ffn(esc, v_lnT):
            hbuf = esc.enter_context(nc.sbuf_tensor("hbuf", [128, 32 * 512], BF16))
            gbuf = esc.enter_context(nc.sbuf_tensor("gbuf", [128, 32 * 512], BF16))
            wbs = [esc.enter_context(nc.sbuf_tensor(f"wb{i}", [128, 8 * 512], BF16))
                   for i in range(2)]
            w3sb = [esc.enter_context(nc.sbuf_tensor(f"w3s{i}", [128, 1024], BF16))
                    for i in range(2)]
            ftmp = esc.enter_context(nc.sbuf_tensor("ftmp", [128, 1024], F32))
            scl = esc.enter_context(nc.sbuf_tensor("scl", [128, 4 * 32], F32))
            s5row = esc.enter_context(nc.sbuf_tensor("s5row", [1, D], F32))
            bbc = esc.enter_context(nc.sbuf_tensor("bbc_f", [128, 1024], F32))
            pb = [esc.enter_context(nc.psum_tensor(f"pfb{i}", [128, 512])) for i in range(8)]
            tg, vg_ = tms.n, vms.n

            def _(a, tg=tg, vg_=vg_):
                a.wait_ge(tms.h, tg)
                a.wait_ge(vms.h, vg_)
                for dst, srcw in [(scl[:, 0:32], s1_in), (scl[:, 32:64], s2_in),
                                  (scl[:, 64:96], s3_in), (scl[:, 96:128], s4_in),
                                  (s5row[:], s5_in)]:
                    a.dma_start(out=dst, in_=srcw).then_inc(aScl.h, 16)
            p.on("scalar", _)
            a_scl = aScl.inc(16 * 5)

            v_w3sbc = 0
            for hb in range(2):
                def _(te, hb=hb, v_w3sbc=v_w3sbc, a_scl=a_scl):
                    te.wait_ge(aScl.h, a_scl)
                    te.wait_ge(sCst.h, s_const)
                    if v_w3sbc:
                        te.wait_ge(vms.h, v_w3sbc)
                    te.matmul(pb[6][:], ones_r, s5row[:, 512 * hb:512 * (hb + 1)],
                              start=True, stop=True).then_inc(tms.h, 1)
                p.on("tensor", _)
                t_w = tms.inc()

                def _(v, hb=hb, t_w=t_w):
                    v.wait_ge(tms.h, t_w)
                    v.tensor_copy(out=bbc[:, 512 * hb:512 * (hb + 1)],
                                  in_=pb[6][:]).then_inc(vms.h, 1)
                p.on("vector", _)
                v_w3sbc = vms.inc()

            nwl = [0]
            wload = []

            def stream_block(src_ap, war_t):
                par = nwl[0] % 2
                nwl[0] += 1

                def _(sy, src_ap=src_ap, war_t=war_t, par=par):
                    sy.wait_ge(sCache.h, g_cache)
                    if war_t:
                        sy.wait_ge(tms.h, war_t)
                    sy.dma_start(out=wbs[par][:].rearrange("p (ic o) -> p ic o", ic=8),
                                 in_=src_ap.rearrange("(ic p) o -> p ic o", p=128)).then_inc(wbsS[par].h, 16)
                p.on("sync", _)
                wload.append(wbsS[par].inc(16))
                return par, wload[-1]

            blk_mm = []
            ep3_prev = 0
            for u in range(4):
                ev_h12 = []
                for wi, (cache, sc_off, fn, dst) in enumerate(
                        [(w1c, 0, AF.Silu, hbuf), (w2c, 32, AF.Identity, gbuf)]):
                    for og in range(8):
                        bi = len(blk_mm)
                        par, s_w = stream_block(cache[og], blk_mm[bi - 2] if bi >= 2 else 0)
                        warv = ev_h12[-1] if len(ev_h12) >= 1 else 0

                        def _(te, par=par, s_w=s_w, og=og, wi=wi, u=u, warv=warv):
                            te.wait_ge(wbsS[par].h, s_w)
                            if og == 0 and wi == 0 and u == 0:
                                te.wait_ge(vms.h, v_lnT)
                            if warv:
                                te.wait_ge(ams.h, warv)
                            for oc in range(4):
                                for ic in range(8):
                                    ins = te.matmul(
                                        pb[oc][:],
                                        wbs[par][:, 512 * ic + 128 * oc:512 * ic + 128 * (oc + 1)],
                                        lnT[:, LQ * ic + 512 * u:LQ * ic + 512 * (u + 1)],
                                        start=(ic == 0), stop=(ic == 7))
                                ins.then_inc(tms.h, 1)
                        p.on("tensor", _)
                        for oc in range(4):
                            tms.inc()
                        blk_mm.append(tms.n)
                        t_base = tms.n - 4

                        def _(a, par=par, og=og, dst=dst, sc_off=sc_off, fn=fn, t_base=t_base,
                              a_scl=a_scl):
                            a.wait_ge(aScl.h, a_scl)
                            for oc in range(4):
                                a.wait_ge(tms.h, t_base + oc + 1)
                                ocp = 4 * og + oc
                                ins = a.activation(out=dst[:, 512 * ocp:512 * (ocp + 1)],
                                                   in_=pb[oc][:], func=fn,
                                                   scale=scl[:, sc_off + ocp:sc_off + ocp + 1])
                            ins.then_inc(ams.h, 1)
                        p.on("scalar", _)
                        ev_h12.append(ams.inc())

                a_w12 = ams.n

                def _(v, a_w12=a_w12):
                    v.wait_ge(ams.h, a_w12)
                    v.tensor_mul(out=hbuf[:], in0=hbuf[:], in1=gbuf[:]).then_inc(vms.h, 1)
                p.on("vector", _)
                v_h = vms.inc()

                prev_stage_a = 0
                for hi, (cache, sc_off, src, dst) in enumerate(
                        [(h1c, 64, hbuf, gbuf), (h2c, 96, gbuf, hbuf)]):
                    ev_hh = []
                    for og in range(8):
                        for icg in range(4):
                            bi = len(blk_mm)
                            par, s_w = stream_block(cache[og][1024 * icg:1024 * (icg + 1), :],
                                                    blk_mm[bi - 2] if bi >= 2 else 0)
                            warv = (ev_hh[-1] if ev_hh else (ev_h12[-1] if hi == 0 else 0))
                            dep_v = v_h if hi == 0 else 0
                            dep_a = prev_stage_a if hi == 1 else 0

                            def _(te, par=par, s_w=s_w, icg=icg, og=og, src=src,
                                  dep_v=dep_v, dep_a=dep_a, warv=warv):
                                te.wait_ge(wbsS[par].h, s_w)
                                if icg == 0 and og == 0:
                                    if dep_v:
                                        te.wait_ge(vms.h, dep_v)
                                    if dep_a:
                                        te.wait_ge(ams.h, dep_a)
                                if icg == 0 and warv:
                                    te.wait_ge(ams.h, warv)
                                for oc in range(4):
                                    for ic8 in range(8):
                                        ic = icg * 8 + ic8
                                        ins = te.matmul(
                                            pb[oc][:],
                                            wbs[par][:, 512 * ic8 + 128 * oc:512 * ic8 + 128 * (oc + 1)],
                                            src[:, 512 * ic:512 * (ic + 1)],
                                            start=(icg == 0 and ic8 == 0),
                                            stop=(icg == 3 and ic8 == 7),
                                            skip_group_check=True)
                                    ins.then_inc(tms.h, 1)
                            p.on("tensor", _)
                            for oc in range(4):
                                tms.inc()
                            blk_mm.append(tms.n)
                        t_base = tms.n - 4

                        def _(a, og=og, dst=dst, sc_off=sc_off, t_base=t_base):
                            for oc in range(4):
                                a.wait_ge(tms.h, t_base + oc + 1)
                                ocp = 4 * og + oc
                                ins = a.activation(out=dst[:, 512 * ocp:512 * (ocp + 1)],
                                                   in_=pb[oc][:], func=AF.Gelu,
                                                   scale=scl[:, sc_off + ocp:sc_off + ocp + 1])
                            ins.then_inc(ams.h, 1)
                        p.on("scalar", _)
                        ev_hh.append(ams.inc())
                    prev_stage_a = ams.n

                a_h2 = ams.n

                w3mm = []
                for th in range(2):
                    ep_guard = ep3_prev
                    for ic32 in range(32):
                        par = ic32 % 2
                        war_w3 = w3mm[-2] if len(w3mm) >= 2 else 0

                        def _(sy, par=par, ic32=ic32, war_w3=war_w3):
                            sy.wait_ge(sCache.h, g_cache)
                            if war_w3:
                                sy.wait_ge(tms.h, war_w3)
                            sy.dma_start(out=w3sb[par][:],
                                         in_=w3c[128 * ic32:128 * (ic32 + 1), :]).then_inc(w3S[par].h, 16)
                        p.on("sync", _)
                        s_wv = w3S[par].inc(16)

                        def _(te, par=par, ic32=ic32, th=th, a_h2=a_h2, s_wv=s_wv,
                              ep_guard=ep_guard):
                            te.wait_ge(w3S[par].h, s_wv)
                            if ic32 == 0:
                                te.wait_ge(ams.h, a_h2)
                                if ep_guard:
                                    te.wait_ge(vms.h, ep_guard)
                            for ti in range(2):
                                tt = th * 2 + ti
                                for n in range(2):
                                    ins = te.matmul(
                                        pb[4 + 2 * ti + n][:],
                                        hbuf[:, 512 * ic32 + 128 * tt:512 * ic32 + 128 * (tt + 1)],
                                        w3sb[par][:, 512 * n:512 * (n + 1)],
                                        start=(ic32 == 0), stop=(ic32 == 31),
                                        skip_group_check=True)
                            ins.then_inc(tms.h, 1)
                        p.on("tensor", _)
                        w3mm.append(tms.inc())
                    t_w3f = w3mm[-1]

                    base_w3 = vms.n

                    def _(v, u=u, th=th, t_w3f=t_w3f, v_w3sbc=v_w3sbc, base_w3=base_w3):
                        v.wait_ge(tms.h, t_w3f)
                        v.wait_ge(vms.h, v_w3sbc)
                        b = base_w3
                        for ti in range(2):
                            gt = 4 * u + th * 2 + ti
                            v.tensor_tensor(out=ftmp[:, 0:512], in0=pb[4 + 2 * ti][:],
                                            in1=bbc[:, 0:512], op=ALU.mult)
                            v.tensor_tensor(out=ftmp[:, 512:1024], in0=pb[4 + 2 * ti + 1][:],
                                            in1=bbc[:, 512:1024], op=ALU.mult).then_inc(vms.h, 1)
                            b += 1
                            v.wait_ge(vms.h, b)
                            v.tensor_add(out=xs[:, 1024 * gt:1024 * gt + 512],
                                         in0=xs[:, 1024 * gt:1024 * gt + 512], in1=ftmp[:, 0:512])
                            v.tensor_add(out=xs[:, 1024 * gt + 512:1024 * (gt + 1)],
                                         in0=xs[:, 1024 * gt + 512:1024 * (gt + 1)],
                                         in1=ftmp[:, 512:1024]).then_inc(vms.h, 1)
                            b += 1
                            if ti == 0:
                                v.wait_ge(vms.h, b)
                    p.on("vector", _)
                    vms.inc(3)
                    ep3_prev = vms.inc()
            return vms.n

        eW0 = es.enter_context(contextlib.ExitStack())
        wO = eW0.enter_context(nc.sbuf_tensor("wO", [128, NCD * D], BF16))
        with contextlib.ExitStack() as e1:
            wA = e1.enter_context(nc.sbuf_tensor("wA", [128, NCD * D], BF16))
            wB = e1.enter_context(nc.sbuf_tensor("wB", [128, NCD * D], BF16))
            wC = e1.enter_context(nc.sbuf_tensor("wC", [128, NCD * D], BF16))
            stage = e1.enter_context(nc.sbuf_tensor("stage0", [128, 2 * LQ], BF16))

            def _(g):
                for dst, srcw in [(wA, wqT0), (wB, wkT0), (wC, wvT0), (wO, woT0)]:
                    g.dma_start(out=dst[:].rearrange("p (c o) -> p c o", c=NCD),
                                in_=srcw.rearrange("(c p) o -> p c o", p=128)).then_inc(gW0.h, 16)
            p.on("gpsimd", _)
            g_wqkv0 = gW0.inc(64)
            g_wo0 = g_wqkv0

            if upto >= 3.9:
                def _(g):
                    for og in range(8):
                        g.dma_start(out=w1c[og], in_=w1b[og]).then_inc(sCache.h, 16)
                        g.dma_start(out=w2c[og], in_=w2b[og]).then_inc(sCache.h, 16)
                    for og in range(8):
                        g.dma_start(out=h1c[og], in_=h1b[og]).then_inc(sCache.h, 16)
                        g.dma_start(out=h2c[og], in_=h2b[og]).then_inc(sCache.h, 16)
                    for q in range(4):
                        g.dma_start(out=w3c[1024 * q:1024 * (q + 1), :],
                                    in_=w3T_in[1024 * q:1024 * (q + 1), :]).then_inc(sCache.h, 16)
                p.on("gpsimd", _)
            g_cache = sCache.inc(16 * 36) if upto >= 3.9 else sCache.n

            s_vin0_box = [0]
            v_ln1 = emit_ln("ln1", e1)
            pps = [e1.enter_context(nc.psum_tensor(f"pp{i}", [128, 512])) for i in range(2)]

            def _(te):
                te.wait_ge(sCst.h, s_const)
                te.wait_ge(aSmall.h, a_small)
                te.matmul(pps[0][:], ones_r, bvrow_sb[0:1, 0:512], start=True, stop=True)
                te.matmul(pps[1][:], ones_r, bvrow_sb[0:1, 512:1024],
                          start=True, stop=True).then_inc(tms.h, 1)
            p.on("tensor", _)
            t_bv = tms.inc()

            def _(v, t_bv=t_bv):
                v.wait_ge(tms.h, t_bv)
                v.tensor_copy(out=bvbc[:, 0:512], in_=pps[0][:])
                v.tensor_copy(out=bvbc[:, 512:1024], in_=pps[1][:]).then_inc(vms.h, 1)
            p.on("vector", _)
            proj_ev.append(vms.inc())
            proj_ev.append(proj_ev[-1])

            kst0, vst0 = [], []
            emit_proj(wA, qTd, biasq[:, 0:NCD], pps, v_ln1, gW0, g_wqkv0, stage)
            emit_proj(wB, kin, biasq[:, NCD:2 * NCD], pps, v_ln1, gW0, g_wqkv0, stage,
                      chunk_stores=kst0)
            s_vin0 = emit_projT(wC, bvbc, pps, v_ln1, gW0, g_wqkv0, stage,
                                chunk_stores=vst0)
            s_vin0_box[0] = s_vin0

        if upto >= 2:
            def _(g, kst0=kst0, vst0=vst0):
                for c in range(8):
                    g.wait_ge(kst0[c][0].h, kst0[c][1])
                    g.collective_compute("AllGather", ALU.bypass,
                                         replica_groups=[[0, 1, 2, 3], [4, 5, 6, 7]],
                                         ins=[kin[128 * c:128 * (c + 1), :]],
                                         outs=[kg8[c]]).then_inc(ccK[0][c].h)
                for c in range(8):
                    for sv in vst0[c]:
                        g.wait_ge(sv[0].h, sv[1])
                    g.collective_compute("AllGather", ALU.bypass,
                                         replica_groups=[[0, 1, 2, 3], [4, 5, 6, 7]],
                                         ins=[vinT[c]],
                                         outs=[vgT8[c]]).then_inc(ccV[0][c].h)
            p.on("gpsimd", _)

            if upto >= 2.5:
                with contextlib.ExitStack() as e2:
                    s_ao0 = emit_attention(0, e2, s_vin0, 0, 0)
        if upto >= 3:
            with contextlib.ExitStack() as e3:
                v_x1 = emit_oproj(e3, wO, s_ao0, gW0, g_wo0, 0)
        eW0.close()
        if upto >= 4:
            with contextlib.ExitStack() as e4:
                v_ln2 = emit_ln("ln2", e4)
            with contextlib.ExitStack() as e5:
                v_x2 = emit_ffn(e5, v_ln2)

        if upto >= 5:
            t_l0_done = tms.n
            v_l0_done = vms.n
            eW1 = es.enter_context(contextlib.ExitStack())
            wO = eW1.enter_context(nc.sbuf_tensor("wO1", [128, NCD * D], BF16))
            with contextlib.ExitStack() as e6:
                wA = e6.enter_context(nc.sbuf_tensor("wA1", [128, NCD * D], BF16))
                wB = e6.enter_context(nc.sbuf_tensor("wB1", [128, NCD * D], BF16))
                wC = e6.enter_context(nc.sbuf_tensor("wC1", [128, NCD * D], BF16))
                stage = e6.enter_context(nc.sbuf_tensor("stage1", [128, 2 * LQ], BF16))

                def _(g, t_l0_done=t_l0_done, v_l0_done=v_l0_done):
                    g.wait_ge(tms.h, t_l0_done)
                    g.wait_ge(vms.h, v_l0_done)
                    for dst, srcw in [(wA, wq1), (wB, wk1), (wC, wv1), (wO, wo1)]:
                        g.dma_start(out=dst[:].rearrange("p (c o) -> p c o", c=NCD),
                                    in_=srcw.rearrange("(c p) o -> p c o", p=128)).then_inc(gW1.h, 16)
                p.on("gpsimd", _)
                g_w1v = gW1.inc(64)

                v_ln3 = emit_ln("ln3", e6)
                pps2 = [e6.enter_context(nc.psum_tensor(f"pq{i}", [128, 512])) for i in range(2)]
                kst1, vst1 = [], []
                emit_proj(wA, qTd, None, pps2, v_ln3, gW1, g_w1v, stage)
                emit_proj(wB, kin, None, pps2, v_ln3, gW1, g_w1v, stage, chunk_stores=kst1)
                s_vin1 = emit_projT(wC, None, pps2, v_ln3, gW1, g_w1v, stage,
                                    chunk_stores=vst1)

        if upto >= 6:
            def _(g, kst1=kst1, vst1=vst1):
                for c in range(8):
                    g.wait_ge(kst1[c][0].h, kst1[c][1])
                    g.collective_compute("AllGather", ALU.bypass,
                                         replica_groups=[[0, 1, 2, 3], [4, 5, 6, 7]],
                                         ins=[kin[128 * c:128 * (c + 1), :]],
                                         outs=[kg8[c]]).then_inc(ccK[1][c].h)
                for c in range(8):
                    for sv in vst1[c]:
                        g.wait_ge(sv[0].h, sv[1])
                    g.collective_compute("AllGather", ALU.bypass,
                                         replica_groups=[[0, 1, 2, 3], [4, 5, 6, 7]],
                                         ins=[vinT[c]],
                                         outs=[vgT8[c]]).then_inc(ccV[1][c].h)
            p.on("gpsimd", _)

            with contextlib.ExitStack() as e7:
                s_ao1 = emit_attention(1, e7, s_vin1, 1, 0)
        if upto >= 7:
            with contextlib.ExitStack() as e8:
                v_x3a = emit_oproj(e8, wO, s_ao1, gW1, g_w1v, 1)

        if upto >= 5:
            eW1.close()
        if upto >= 8:
         with contextlib.ExitStack() as e9:
            v_ln4 = emit_ln("ln4", e9)
            weff = e9.enter_context(nc.sbuf_tensor("weff", [128, NCD * D], BF16))
            lsc = e9.enter_context(nc.sbuf_tensor("lsc", [128, 2048], BF16))
            ltmp = e9.enter_context(nc.sbuf_tensor("ltmp", [128, 1024], F32))
            lw = e9.enter_context(nc.sbuf_tensor("lw", [128, 1128], BF16))
            ldax = e9.enter_context(nc.sbuf_tensor("ldax", [128, 1024], BF16))
            bbc = e9.enter_context(nc.sbuf_tensor("bbc_l1", [128, 1024], F32))
            brow_o = e9.enter_context(nc.sbuf_tensor("brow_l1", [1, D], F32))
            ppf = [e9.enter_context(nc.psum_tensor(f"pf{i}", [128, 512])) for i in range(4)]
            tg9, vg9 = tms.n, vms.n

            def _(a, tg9=tg9, vg9=vg9):
                a.wait_ge(tms.h, tg9)
                a.wait_ge(vms.h, vg9)
                a.dma_start(out=brow_o[:], in_=b1_in).then_inc(aB1.h, 16)
            p.on("scalar", _)
            a_b1 = aB1.inc(16)

            def _(g, tg9=tg9, vg9=vg9):
                g.wait_ge(tms.h, tg9)
                g.wait_ge(vms.h, vg9)
                g.dma_start(out=weff[:].rearrange("p (c o) -> p c o", c=NCD),
                            in_=wt1.rearrange("(c p) o -> p c o", p=128)).then_inc(gLW.h, 16)
                g.dma_start(out=lw[0:50, 0:1024], in_=uat).then_inc(gLW.h, 16)
                g.dma_start(out=lw[0:50, 1024:1028], in_=lup).then_inc(gLW.h, 16)
                g.dma_start(out=lw[0:4, 1028:1128], in_=ldn).then_inc(gLW.h, 16)
                g.dma_start(out=ldax[0:100, :], in_=dax).then_inc(gLW.h, 16)
            p.on("gpsimd", _)
            g_lw = gLW.inc(80)

            def _(te, g_lw=g_lw):
                te.wait_ge(gLW.h, g_lw)
                te.matmul(ppf[0][0:4, :], lw[0:50, 1024:1028], lw[0:50, 0:512], start=True, stop=True)
                te.matmul(ppf[1][0:4, :], lw[0:50, 1024:1028], lw[0:50, 512:1024],
                          start=True, stop=True).then_inc(tms.h, 1)
            p.on("tensor", _)
            t_t1 = tms.inc()

            def _(v, t_t1=t_t1):
                v.wait_ge(tms.h, t_t1)
                v.tensor_copy(out=lsc[0:4, 0:512], in_=ppf[0][0:4, :])
                v.tensor_copy(out=lsc[0:4, 512:1024], in_=ppf[1][0:4, :]).then_inc(vms.h, 1)
            p.on("vector", _)
            v_t1 = vms.inc()

            def _(te, v_t1=v_t1):
                te.wait_ge(vms.h, v_t1)
                te.matmul(ppf[2][0:100, :], lw[0:4, 1028:1128], lsc[0:4, 0:512], start=True, stop=True)
                te.matmul(ppf[3][0:100, :], lw[0:4, 1028:1128], lsc[0:4, 512:1024],
                          start=True, stop=True).then_inc(tms.h, 1)
            p.on("tensor", _)
            t_t2 = tms.inc()

            def _(v, t_t2=t_t2):
                v.wait_ge(tms.h, t_t2)
                v.tensor_copy(out=lsc[0:100, 1024:1536], in_=ppf[2][0:100, :])
                v.tensor_copy(out=lsc[0:100, 1536:2048], in_=ppf[3][0:100, :]).then_inc(vms.h, 1)
            p.on("vector", _)
            v_t2 = vms.inc()

            v_weff = 0
            for c in range(NCD):
                def _(te, c=c, v_t2=v_t2, v_weff=v_weff):
                    if c == 0:
                        te.wait_ge(vms.h, v_t2)
                    if c >= 2:
                        te.wait_ge(vms.h, v_weff)
                    te.matmul(ppf[c % 2][:], ldax[0:100, 128 * c:128 * (c + 1)],
                              lsc[0:100, 1024:1536], start=True, stop=True)
                    te.matmul(ppf[2 + c % 2][:], ldax[0:100, 128 * c:128 * (c + 1)],
                              lsc[0:100, 1536:2048], start=True, stop=True).then_inc(tms.h, 1)
                p.on("tensor", _)
                t_t3 = tms.inc()

                def _(v, c=c, t_t3=t_t3):
                    v.wait_ge(tms.h, t_t3)
                    v.tensor_add(out=weff[:, D * c:D * c + 512],
                                 in0=weff[:, D * c:D * c + 512], in1=ppf[c % 2][:])
                    v.tensor_add(out=weff[:, D * c + 512:D * (c + 1)],
                                 in0=weff[:, D * c + 512:D * (c + 1)],
                                 in1=ppf[2 + c % 2][:]).then_inc(vms.h, 1)
                p.on("vector", _)
                v_weff = vms.inc()

            v_b1bc = 0
            for hb in range(2):
                def _(te, hb=hb, v_weff=v_weff, v_b1bc=v_b1bc, a_b1=a_b1):
                    te.wait_ge(vms.h, v_weff if hb == 0 else v_b1bc)
                    te.wait_ge(aB1.h, a_b1)
                    te.matmul(ppf[hb][:], ones_r, brow_o[0:1, 512 * hb:512 * (hb + 1)],
                              start=True, stop=True).then_inc(tms.h, 1)
                p.on("tensor", _)
                t_bb = tms.inc()

                def _(v, hb=hb, t_bb=t_bb):
                    v.wait_ge(tms.h, t_bb)
                    v.tensor_copy(out=bbc[:, 512 * hb:512 * (hb + 1)],
                                  in_=ppf[hb][:]).then_inc(vms.h, 1)
                p.on("vector", _)
                v_b1bc = vms.inc()

            fin_ep = []
            for tt in range(NT):
                banks = [ppf[0], ppf[1]] if tt % 2 == 0 else [ppf[2], ppf[3]]
                warv = fin_ep[tt - 2] if tt >= 2 else 0

                def _(te, tt=tt, banks=banks, v_ln4=v_ln4, v_b1bc=v_b1bc, warv=warv):
                    if tt == 0:
                        te.wait_ge(vms.h, max(v_ln4, v_b1bc))
                    if warv:
                        te.wait_ge(vms.h, warv)
                    for n in range(2):
                        for ic in range(NCD):
                            ins = te.matmul(
                                banks[n][:],
                                lnT[:, LQ * ic + 128 * tt:LQ * ic + 128 * (tt + 1)],
                                weff[:, D * ic + 512 * n:D * ic + 512 * (n + 1)],
                                start=(ic == 0), stop=(ic == 7))
                    ins.then_inc(tms.h, 1)
                p.on("tensor", _)
                t_f = tms.inc()

                fin_base = vms.n
                prev_fin = fin_ep[-1] if fin_ep else 0

                def _(v, tt=tt, banks=banks, t_f=t_f, v_b1bc=v_b1bc, fin_base=fin_base,
                      prev_fin=prev_fin):
                    v.wait_ge(tms.h, t_f)
                    if prev_fin:
                        v.wait_ge(vms.h, prev_fin)
                    v.tensor_add(out=ltmp[:, 0:512], in0=banks[0][:], in1=bbc[:, 0:512])
                    v.tensor_add(out=ltmp[:, 512:1024], in0=banks[1][:],
                                 in1=bbc[:, 512:1024]).then_inc(vms.h, 1)
                    v.wait_ge(vms.h, fin_base + 1)
                    v.tensor_add(out=xs[:, 1024 * tt:1024 * tt + 512],
                                 in0=xs[:, 1024 * tt:1024 * tt + 512], in1=ltmp[:, 0:512])
                    v.tensor_add(out=xs[:, 1024 * tt + 512:1024 * (tt + 1)],
                                 in0=xs[:, 1024 * tt + 512:1024 * (tt + 1)],
                                 in1=ltmp[:, 512:1024]).then_inc(vms.h, 1)
                p.on("vector", _)
                vms.inc(1)
                fin_ep.append(vms.inc())
            pass

        v_final = vms.n

        def _(sy, v_final=v_final):
            sy.wait_ge(vms.h, v_final)
            sy.dma_start(out=out_d.rearrange("(t p) i -> p t i", p=128),
                         in_=sb_x3).then_inc(sOut.h, 16)
        p.on("sync", _)
        st_out = sOut.inc(16)

        if "lnT" in dump_outs:
            def _(g, dst=dump_outs["lnT"]):
                g.wait_ge(sOut.h, st_out)
                g.dma_start(out=dst.rearrange("p (c q) -> p c q", c=NCD) if False else dst,
                            in_=lnT[:]).then_inc(gLW.h, 16)
            p.on("gpsimd", _)
            gLW.inc(16)
        n_dumps = len(dump_outs)
        if n_dumps:
            srcmap = {"qT": qTd, "aoT": aoTd, "vin": vinT}
            gd_base = gLW.n

            def _(g, gd_base=gd_base):
                g.wait_ge(sOut.h, st_out)
                k = 0
                for name, dst in dump_outs.items():
                    if name not in srcmap:
                        continue
                    g.dma_start(out=dst, in_=srcmap[name]).then_inc(gLW.h, 16)
                    k += 1
                g.wait_ge(gLW.h, gd_base + 16 * k)
            p.on("gpsimd", _)
            for name in dump_outs:
                if name in ("qT", "aoT", "vin"):
                    gLW.inc(16)

        def _(sy):
            sy.wait_ge(sOut.h, st_out)
        p.on("sync", _)

        block = es.enter_context(nc.Block())
        p.materialize(block)
    return nc


def _prep_inputs(inputs):
    f32 = np.float32
    g = lambda k: np.asarray(inputs[k], dtype=f32)
    shared = {}
    shared["wqT0"] = np.ascontiguousarray(g("L0_attn_Wq").T)
    shared["wkT0"] = np.ascontiguousarray(g("L0_attn_Wk").T)
    shared["wvT0"] = np.ascontiguousarray(g("L0_attn_Wv").T)
    shared["woT0"] = np.ascontiguousarray(g("L0_attn_Wo").T)
    shared["bq0"] = np.ascontiguousarray(g("L0_attn_bq").reshape(8, 128).T)
    shared["bk0"] = np.ascontiguousarray(g("L0_attn_bk").reshape(8, 128).T)
    shared["bv0"] = np.ascontiguousarray(g("L0_attn_bv").reshape(8, 128).T)
    shared["bvr0"] = g("L0_attn_bv").reshape(1, D)
    shared["bo0"] = g("L0_attn_bo").reshape(1, D)
    gperm = np.concatenate([g("L0_attn_gamma")[0::2], g("L0_attn_gamma")[1::2]])
    shared["gamma"] = np.ascontiguousarray(gperm.reshape(1, 16))

    def blk(w):
        wt = np.ascontiguousarray(w.T)
        return np.ascontiguousarray(
            wt.reshape(wt.shape[0], 8, wt.shape[1] // 8).transpose(1, 0, 2))

    shared["w1b"] = blk(g("L0_ffn_w1_w"))
    shared["w2b"] = blk(g("L0_ffn_w2_w"))
    shared["h1b"] = blk(g("L0_ffn_h1_w"))
    shared["h2b"] = blk(g("L0_ffn_h2_w"))
    shared["w3T"] = np.ascontiguousarray(g("L0_ffn_w3_w").T)
    shared["s1"] = np.ascontiguousarray(g("L0_ffn_w1_s").reshape(32, 128).T)
    shared["s2"] = np.ascontiguousarray(g("L0_ffn_w2_s").reshape(32, 128).T)
    shared["s3"] = np.ascontiguousarray(g("L0_ffn_h1_s").reshape(32, 128).T)
    shared["s4"] = np.ascontiguousarray(g("L0_ffn_h2_s").reshape(32, 128).T)
    shared["s5"] = g("L0_ffn_w3_s").reshape(1, D)
    shared["wq1"] = np.ascontiguousarray(g("L1_attn_Wq"))
    shared["wk1"] = np.ascontiguousarray(g("L1_attn_Wk"))
    shared["wv1"] = np.ascontiguousarray(g("L1_attn_Wv"))
    shared["wo1"] = np.ascontiguousarray(g("L1_attn_Wo"))
    shared["bo1"] = g("L1_attn_bo").reshape(1, D)
    shared["wt1"] = np.ascontiguousarray(g("L1_ffn_W").T)
    shared["uat"] = np.ascontiguousarray(g("L1_ffn_up_aux").T)
    shared["lup"] = np.ascontiguousarray(g("L1_ffn_lora_up"))
    shared["ldn"] = np.ascontiguousarray(g("L1_ffn_lora_down"))
    shared["dax"] = np.ascontiguousarray(g("L1_ffn_down_aux"))
    shared["b1"] = g("L1_ffn_b").reshape(1, D)

    x = g("L0_attn_x")
    tri = np.triu(np.ones((128, 128), np.float32))
    in_maps = []
    row_idx = []
    for p in range(8):
        s, r = p // 4, p % 4
        rows = np.concatenate([np.arange(128 * (4 * lt + r), 128 * (4 * lt + r) + 128)
                               for lt in range(NT)])
        row_idx.append((s, rows))
        xsh = np.ascontiguousarray(x[s][rows])
        qpos = rows.astype(np.float32)
        qpd = np.ascontiguousarray(np.broadcast_to(qpos[None, :], (128, LQ)))
        masks = np.zeros((16, 128, 512), np.float32)
        for m in range(16):
            for mp in range(4):
                if m < 4 * mp + r:
                    masks[m, :, 128 * mp:128 * (mp + 1)] = 1.0
                elif m == 4 * mp + r:
                    masks[m, :, 128 * mp:128 * (mp + 1)] = tri
        im = {"x": xsh, "qpd": qpd, "masks": masks}
        im.update(shared)
        in_maps.append(im)
    return in_maps, row_idx


def kernel(**inputs):
    in_maps, row_idx = _prep_inputs(inputs)
    nc = build_program()
    res = run_bass_kernel_spmd(nc, in_maps, core_ids=list(range(8)))
    out = np.zeros((2, SEQ, D), np.float32)
    for p in range(8):
        s, rows = row_idx[p]
        out[s][rows] = res.results[p]["out"]
    return out
